# version 08
# baseline (speedup 1.0000x reference)
import numpy as np

import concourse.bass as bass
import concourse.mybir as mybir
from concourse.bass_utils import run_bass_kernel_spmd

F32 = mybir.dt.float32
BF16 = mybir.dt.bfloat16
AF = mybir.ActivationFunctionType
ALU = mybir.AluOpType
AX = mybir.AxisListType

T = 2048
DM = 1024
NTC = 16
NDC = 8
NHEAD = 16
EPS = 1e-5
NCORES = 8

SBUF_BASE = 16512
SBUF_TOP = 229344


def _l0_col_perm():
    perm = np.zeros(1024, dtype=np.int64)
    for h in range(16):
        for i in range(32):
            perm[64 * h + i] = 64 * h + 2 * i
            perm[64 * h + 32 + i] = 64 * h + 2 * i + 1
    return perm


def _rope_tables_l0():
    inv = 1.0 / (10000.0 ** (np.arange(0, 64, 2, dtype=np.float32) / 64))
    ang = np.outer(np.arange(T, dtype=np.float32), inv)
    cos, sin = np.cos(ang), np.sin(ang)
    cc = np.zeros((128, T), dtype=np.float32)
    ss = np.zeros((128, T), dtype=np.float32)
    for r in range(128):
        i = r % 32
        cc[r] = cos[:, i]
        ss[r] = sin[:, i] if (r % 64) >= 32 else -sin[:, i]
    return cc, ss


def _rope_tables_l1():
    inv = 10000.0 ** (-np.arange(0, 32, 2, dtype=np.float32) / 32)
    ang = np.arange(T, dtype=np.float32)[:, None] * inv[None, :]
    cos, sin = np.cos(ang), np.sin(ang)
    cc = np.zeros((128, T), dtype=np.float32)
    ss = np.zeros((128, T), dtype=np.float32)
    for r in range(128):
        idx = r % 32
        i = idx % 16
        cc[r] = cos[:, i]
        ss[r] = sin[:, i] if idx >= 16 else -sin[:, i]
    return cc, ss


def _perm0():
    p = np.zeros((128, 128), dtype=np.float32)
    for i in range(128):
        j = (i // 64) * 64 + ((i + 32) % 64)
        p[j, i] = 1.0
    return p


def _perm1():
    p = np.zeros((128, 128), dtype=np.float32)
    for base in (0, 64):
        for i in range(16):
            p[base + 16 + i, base + i] = 1.0
            p[base + i, base + 16 + i] = 1.0
    return p


def _sel4():
    s = np.zeros((32, 128), dtype=np.float32)
    for j in range(4):
        for p_ in range(32):
            s[p_, 32 * j + p_] = 1.0
    return s


def _scale_fm(s):
    c = s.shape[0] // 128
    return np.ascontiguousarray(s.reshape(c, 128).T)


def host_prep(inputs):
    cp = np.ascontiguousarray
    perm0 = _l0_col_perm()
    cc0, ss0 = _rope_tables_l0()
    cc1, ss1 = _rope_tables_l1()

    wqrT = inputs["L1_attn_W_QR"].T
    wqr_pad = np.zeros((512, 1024), dtype=np.float32)
    for h in range(16):
        base = (h // 2) * 128 + (h % 2) * 64
        for i in range(16):
            wqr_pad[:, base + i] = wqrT[:, 32 * h + 2 * i]
            wqr_pad[:, base + 16 + i] = wqrT[:, 32 * h + 2 * i + 1]

    wkrT = inputs["L1_attn_W_KR"].T
    wkr_pad = np.zeros((1024, 128), dtype=np.float32)
    for i in range(16):
        wkr_pad[:, i] = wkrT[:, 2 * i]
        wkr_pad[:, 16 + i] = wkrT[:, 2 * i + 1]

    eb = inputs["L1_ffn_expert_b"]

    common = {
        "ident": np.eye(128, dtype=np.float32),
        "cc0": cc0, "ss0": ss0, "cc1": cc1, "ss1": ss1,
        "perm0m": _perm0(), "perm1m": _perm1(),
        "tril": np.triu(np.ones((128, 128), dtype=np.float32)),
        "ones1": np.ones((1, 128), dtype=np.float32),
        "sel4": _sel4(),
        "wq": cp(inputs["L0_attn_wq"][:, perm0]),
        "wdn": cp(inputs["L0_attn_w_down"]),
        "wku": cp(inputs["L0_attn_wk_up"][:, perm0]),
        "wvu": cp(inputs["L0_attn_wv_up"]),
        "wo": cp(inputs["L0_attn_wo"]),
        "w1t": cp(inputs["L0_ffn_w1_w"].T), "s1": _scale_fm(inputs["L0_ffn_w1_s"]),
        "w2t": cp(inputs["L0_ffn_w2_w"].T), "s2": _scale_fm(inputs["L0_ffn_w2_s"]),
        "h1t": cp(inputs["L0_ffn_h1_w"].T), "hs1": _scale_fm(inputs["L0_ffn_h1_s"]),
        "h2t": cp(inputs["L0_ffn_h2_w"].T), "hs2": _scale_fm(inputs["L0_ffn_h2_s"]),
        "w3t": cp(inputs["L0_ffn_w3_w"].T),
        "s3fm": _scale_fm(inputs["L0_ffn_w3_s"]),
        "wdq": cp(inputs["L1_attn_W_DQ"].T),
        "wuq": cp(inputs["L1_attn_W_UQ"].T),
        "wqrp": wqr_pad,
        "wdkv": cp(inputs["L1_attn_W_DKV"].T),
        "wuk": cp(inputs["L1_attn_W_UK"].T),
        "wuv": cp(inputs["L1_attn_W_UV"].T),
        "wkrp": wkr_pad,
        "wot": cp(inputs["L1_attn_W_O"].T),
        "phit": cp(inputs["L1_ffn_phi_w"].T),
        "phib": cp(inputs["L1_ffn_phi_b"][None, :]),
    }
    per_core = []
    for c in range(NCORES):
        ew = cp(inputs["L1_ffn_expert_w"][8 * c: 8 * c + 8]).reshape(8192, 1024)
        ebs = eb[8 * c: 8 * c + 8]
        eb_fm = np.concatenate(
            [np.ascontiguousarray(ebs[e].reshape(8, 128).T) for e in range(8)], axis=1)
        per_core.append({
            "x": cp(inputs["L0_attn_x"][c]),
            "ew": ew,
            "ebfm": cp(eb_fm),
        })
    return common, per_core


ENGS = ("sync", "vector", "scalar", "tensor", "gpsimd")


class K:
    def __init__(self, nc):
        self.nc = nc
        self.q = {e: [] for e in ENGS}
        self.sems = {}
        self.cnt = {}

    def sem(self, n):
        if n not in self.sems:
            self.sems[n] = self.nc.alloc_semaphore(n)
            self.cnt[n] = 0
        return self.sems[n]

    def tok(self, n):
        self.sem(n)
        return self.cnt[n]

    def op(self, eng, build, wait=(), inc=None, inc_by=1, drain=False):
        tok = None
        if inc is not None:
            self.sem(inc)
            self.cnt[inc] += inc_by
            tok = self.cnt[inc]
        sem = self.sems[inc] if inc is not None else None
        for (s, t) in wait:
            self.sem(s)

        def fn(E, build=build, wait=tuple(wait), sem=sem, inc_by=inc_by, drain=drain):
            if drain:
                E.drain()
            for (s, t) in wait:
                if t > 0:
                    E.wait_ge(self.sems[s], t)
            ins = build(E)
            if sem is not None:
                ins.then_inc(sem, inc_by)

        self.q[eng].append(fn)
        return tok

    def dma(self, eng, out, in_, wait=(), sem=None, drain=False):
        sem = sem or ("sg" if eng == "gpsimd" else ("sa" if eng == "scalar" else "sd"))
        return self.op(eng, lambda E: E.dma_start(out=out, in_=in_),
                       wait=wait, inc=sem, inc_by=16, drain=drain)

    def flush(self):
        with self.nc.Block() as blk:
            for name in ENGS:
                if not self.q[name]:
                    continue
                fns = self.q[name]
                self.q[name] = []

                def body(E, fns=fns):
                    for f in fns:
                        f(E)
                getattr(blk, name)(body)


class Arena:

    def __init__(self, nc, base=SBUF_BASE):
        self.nc = nc
        self.off = base
        self.offs = {}

    @staticmethod
    def bpp(shape, dtype):
        n = 1
        for s in shape[1:]:
            n *= s
        return n * (2 if dtype in (BF16, mybir.dt.float16) else 4)

    def alloc(self, name, shape, dtype, align=32):
        off = (self.off + align - 1) // align * align
        h = self.nc.alloc_sbuf_tensor_at(name, shape, dtype, offset=off)
        self.offs[name] = off
        self.off = off + self.bpp(shape, dtype)
        assert self.off <= SBUF_TOP, f"SBUF overflow at {name}: {self.off}"
        return h

    def at(self, name, shape, dtype, offset):
        assert offset + self.bpp(shape, dtype) <= SBUF_TOP, name
        self.offs[name] = offset
        return self.nc.alloc_sbuf_tensor_at(name, shape, dtype, offset=offset)


class LNBufs:

    def __init__(self, a, tag):
        self.rb = [a.alloc(f"{tag}_rb{i}", [128, 1024], F32) for i in range(2)]
        self.st = a.alloc(f"{tag}_st", [128, 12], F32)
        self.mv = a.alloc(f"{tag}_mv", [128, 2], F32)
        self.rstd = a.alloc(f"{tag}_rstd", [128, 2], F32)
        self.ytm = [a.alloc(f"{tag}_ytm{i}", [128, 1024], BF16) for i in range(2)]
        self.ob = a.at(f"{tag}_ob", [128, 1024], F32, a.offs[f"{tag}_ytm0"])
        self.eps = a.alloc(f"{tag}_eps", [128, 1], F32)


def fm_view(h, dc_count, tcols, dc, csl):
    return h[:, dc * tcols + csl.start: dc * tcols + csl.stop]


def emit_ln_stream(k, b, psT, r_dram, y_fm, y_tm, ident_b, row0=0, nchunks=NTC,
                   first_wait=(), rb_guard=None):
    fw = list(first_wait)
    evac_tok = [0, 0]
    rb_guard = list(rb_guard) if rb_guard else [None, None]
    tr_evac = [0, 0]
    use_tok = [0, 0]
    ntok = 128 * nchunks
    for tc in range(nchunks):
        p = tc % 2
        r0 = row0 + tc * 128
        ld_wait = [("dve", evac_tok[p])]
        if rb_guard[p] is not None:
            ld_wait.append(rb_guard[p])
            rb_guard[p] = None
        t_load = k.dma("sync", b.rb[p][:, :], r_dram[r0:r0 + 128, :],
                       wait=ld_wait, sem=f"rb{p}")
        k.op("vector", lambda E, p=p: E.bn_stats(b.st[:, 0:6], b.rb[p][:, 0:512]),
             wait=[(f"rb{p}", t_load)], drain=True)
        k.op("vector", lambda E, p=p: E.bn_stats(b.st[:, 6:12], b.rb[p][:, 512:1024]))
        t_st = k.op("vector", lambda E: E.bn_aggr(b.mv[:, 0:2], b.st[:, :]),
                    inc="dve", drain=True)
        k.op("scalar", lambda E: E.activation(b.rstd[:, 0:1], b.mv[:, 1:2], AF.Ln,
                                              bias=b.eps[:, 0:1]),
             wait=[("dve", t_st)])
        t_rs = k.op("scalar", lambda E: E.activation(b.rstd[:, 1:2], b.rstd[:, 0:1],
                                                     AF.Exp, scale=-0.5),
                    inc="act", drain=True)
        dst = y_tm[:, tc * 1024:(tc + 1) * 1024] if y_tm is not None else b.ytm[p][:, :]
        t_app = k.op("vector",
                     lambda E, p=p, dst=dst: E.tensor_scalar(
                         dst, b.rb[p][:, :], b.mv[:, 0:1], b.rstd[:, 1:2],
                         ALU.subtract, ALU.mult),
                     wait=[("act", t_rs), ("pe", use_tok[p])], inc="dve")
        evac_tok[p] = t_app
        ps = psT[p]
        t_tr = None
        for dc in range(NDC):
            wl = []
            if dc == 0:
                wl = [("dve", t_app), ("dve", tr_evac[p])] + fw
                fw = []
            t_tr = k.op("tensor",
                        lambda E, dc=dc, ps=ps, src=dst: E.transpose(
                            ps[:, dc * 128:(dc + 1) * 128], src[:, dc * 128:(dc + 1) * 128],
                            ident_b[:, :]),
                        wait=wl, inc=("pe" if dc == NDC - 1 else None))
        use_tok[p] = t_tr

        def evac(E, ps=ps, tc=tc):
            dst_ap = y_fm[:, :].rearrange("q (a t) -> q a t", a=8)[:, :, tc * 128:tc * 128 + 128]
            return E.tensor_copy(dst_ap, ps[:, :].rearrange("q (a t) -> q a t", a=8))
        tr_evac[p] = k.op("vector", evac, wait=[("pe", t_tr)], inc="dve")
    return k.tok("dve")


def wsrc(dr_t):
    return dr_t.ap().rearrange("(c p) m -> p c m", p=128)


def wdst(wbuf, kchunks, mtot):
    return wbuf[:, 0:kchunks * mtot].rearrange("p (c m) -> p c m", c=kchunks)


def emit_proj(k, wbuf_sb, kchunks, mchunks, rhs_fm, rhs_tcols, out_sb, out_tcols,
              psums, n_ttiles, ttile, extra_wait=()):
    toks = [k.tok("dve")] * len(psums)
    first = True
    for mc in range(mchunks):
        for tt in range(n_ttiles):
            i = (mc * n_ttiles + tt) % len(psums)
            ps = psums[i]
            t_mm = None
            for kc in range(kchunks):
                w_sl = wbuf_sb[:, (kc * mchunks + mc) * 128:(kc * mchunks + mc) * 128 + 128]
                r_sl = rhs_fm[:, kc * rhs_tcols + tt * ttile: kc * rhs_tcols + tt * ttile + ttile]
                wl = []
                if kc == 0:
                    wl = [("dve", toks[i])]
                    if first:
                        wl += list(extra_wait)
                t_mm = k.op("tensor",
                            lambda E, ps=ps, w_sl=w_sl, r_sl=r_sl, kc=kc, ttile=ttile:
                            E.matmul(ps[:, 0:ttile], w_sl, r_sl,
                                     start=(kc == 0), stop=(kc == kchunks - 1)),
                            wait=wl, inc=("pe" if kc == kchunks - 1 else None))
            first = False
            dst = out_sb[:, mc * out_tcols + tt * ttile: mc * out_tcols + tt * ttile + ttile]
            toks[i] = k.op("vector",
                           lambda E, dst=dst, ps=ps, ttile=ttile:
                           E.tensor_copy(dst, ps[:, 0:ttile]),
                           wait=[("pe", t_mm)], inc="dve")
    return k.tok("dve")


def emit_rope(k, tgt, nchunks, cc_b, ss_b, perm_b, pst, scr, first_wait=(),
              ps_free=0):
    first = True
    for ch in range(nchunks):
        for j4 in range(4):
            sl = slice(ch * T + j4 * 512, ch * T + j4 * 512 + 512)
            tsl = slice(j4 * 512, j4 * 512 + 512)
            wl = [("dve", ps_free)]
            if first:
                wl += list(first_wait)
                first = False
            t_sw = k.op("tensor",
                        lambda E, sl=sl: E.matmul(pst[:, :], perm_b[:, :], tgt[:, sl],
                                                  start=True, stop=True),
                        wait=wl, inc="pe")
            k.op("vector", lambda E, tsl=tsl: E.tensor_tensor(
                scr[:, 0:512], pst[:, :], ss_b[:, tsl], ALU.mult),
                wait=[("pe", t_sw)])
            k.op("vector", lambda E, sl=sl, tsl=tsl: E.tensor_tensor(
                scr[:, 512:1024], tgt[:, sl], cc_b[:, tsl], ALU.mult))
            ps_free = k.op("vector", lambda E, sl=sl: E.tensor_tensor(
                tgt[:, sl], scr[:, 0:512], scr[:, 512:1024], ALU.add),
                inc="dve", drain=True)
    return ps_free


def emit_epilogue_chunk(k, b, tc, psum_pair, scale_fn, r_dram, out_dram,
                        wait_pe, radd_free):
    p = tc % 2
    t_ld = k.dma("sync", b.rb[p][:, :], r_dram[tc * 128:(tc + 1) * 128, :],
                 wait=[(f"st{p}", radd_free[p])], sem=f"rb{p}")
    for half in range(2):
        k.op("vector",
             lambda E, half=half: scale_fn(E, b.ob[:, half * 512:half * 512 + 512],
                                           psum_pair[half], half),
             wait=([("pe", wait_pe)] if half == 0 else []), drain=(half == 0))
    t_ev = k.op("vector", lambda E: E.bn_stats(b.st[:, 0:6], b.ob[:, 0:512]),
                inc="dve", drain=True)
    k.op("vector", lambda E: E.bn_stats(b.st[:, 6:12], b.ob[:, 512:1024]))
    t_st = k.op("vector", lambda E: E.bn_aggr(b.mv[:, 0:2], b.st[:, :]),
                inc="dve", drain=True)
    k.op("scalar", lambda E: E.activation(b.rstd[:, 0:1], b.mv[:, 1:2], AF.Ln,
                                          bias=b.eps[:, 0:1]),
         wait=[("dve", t_st)])
    t_rs = k.op("scalar", lambda E: E.activation(b.rstd[:, 1:2], b.rstd[:, 0:1],
                                                 AF.Exp, scale=-0.5),
                inc="act", drain=True)
    k.op("vector", lambda E: E.tensor_scalar(b.ob[:, :], b.ob[:, :], b.mv[:, 0:1],
                                             b.rstd[:, 1:2], ALU.subtract, ALU.mult),
         wait=[("act", t_rs), (f"rb{p}", t_ld)])
    t_add = k.op("vector", lambda E, p=p: E.tensor_tensor(
        b.rb[p][:, :], b.ob[:, :], b.rb[p][:, :], ALU.add), inc="dve", drain=True)
    t_out = k.dma("sync", out_dram[tc * 128:(tc + 1) * 128, :], b.rb[p][:, :],
                  wait=[("dve", t_add)], sem=f"st{p}")
    radd_free[p] = t_out
    return t_ev


def emit_attn0(nc, k, dr, ps, pt, pw):
    a = Arena(nc)
    ident_b = a.alloc("A_ident", [128, 128], BF16)
    cc0_b = a.alloc("A_cc0", [128, T], BF16)
    ss0_b = a.alloc("A_ss0", [128, T], BF16)
    perm_b = a.alloc("A_perm", [128, 128], BF16)
    v_tm = a.alloc("A_v", [128, NTC * 1040], BF16)
    q_sb = a.alloc("A_q", [128, 8 * T], BF16)
    k_sb = a.alloc("A_k", [128, 8 * T], BF16)
    attnout = a.alloc("A_ao", [128, NTC * 1024], BF16)
    lat = a.at("A_lat", [128, 4 * T], BF16, a.offs["A_ao"])
    ov1 = a.off
    y0_fm = a.at("A_y0fm", [128, 8 * T], BF16, ov1)
    ebuf = [a.at(f"A_e{i}", [128, 16 * 512], BF16, ov1 + i * 16 * 512 * 2) for i in range(2)]
    ao_fm = a.at("A_aofm", [128, 8 * T], BF16, ov1)
    a.off = ov1 + 8 * T * 2
    wbuf = a.alloc("A_wbuf", [128, 8192], BF16)
    scr = a.alloc("A_scr", [128, 1024], BF16)
    rcp = a.alloc("A_rcp", [128, 2], F32)
    b = LNBufs(a, "A")


    for nm, dst in [("ident", ident_b), ("cc0", cc0_b), ("ss0", ss0_b), ("perm0m", perm_b)]:
        k.dma("gpsimd", dst[:, :], dr[nm].ap(), sem="scst")
    t_consts = k.tok("scst")
    t_wq = k.dma("gpsimd", wdst(wbuf, 8, 1024), wsrc(dr["wq"]))
    t_ones = k.op("vector", lambda E: E.memset(
        v_tm[:, :].rearrange("q (h c) -> q h c", c=65)[:, :, 64:65], 1.0), inc="dve")
    k.op("vector", lambda E: E.memset(b.eps[:, :], EPS))
    emit_ln_stream(k, b, pt, dr["x"].ap(), y0_fm, None, ident_b,
                   first_wait=[("scst", t_consts)])
    k.flush()

    emit_proj(k, wbuf, 8, 8, y0_fm, T, q_sb, T, ps[0:2], 4, 512,
              extra_wait=[("sg", t_wq)])
    t_qdone = k.tok("dve")
    t_wdn = k.dma("gpsimd", wdst(wbuf, 8, 512), wsrc(dr["wdn"]),
                  wait=[("dve", t_qdone)])
    emit_proj(k, wbuf, 8, 4, y0_fm, T, lat, T, ps[2:4], 4, 512,
              extra_wait=[("sg", t_wdn)])
    t_latdone = k.tok("dve")
    t_wku = k.dma("gpsimd", wdst(wbuf, 4, 1024), wsrc(dr["wku"]),
                  wait=[("dve", t_latdone)])
    emit_proj(k, wbuf, 4, 8, lat, T, k_sb, T, ps[0:2], 4, 512,
              extra_wait=[("sg", t_wku), ("dve", t_latdone)])
    t_kdone = k.tok("dve")
    t_wvu = k.dma("gpsimd", wdst(wbuf, 4, 1024), wsrc(dr["wvu"]),
                  wait=[("dve", t_kdone)])
    vp_free = [k.tok("dve")] * 2
    for tc in range(NTC):
        for n2 in range(2):
            i = n2
            t_mm = None
            for kc in range(4):
                wl = []
                if kc == 0:
                    wl = [("dve", vp_free[i])]
                    if tc == 0 and n2 == 0:
                        wl += [("sg", t_wvu), ("dve", t_ones)]
                t_mm = k.op("tensor",
                            lambda E, i=i, kc=kc, tc=tc, n2=n2: E.matmul(
                                ps[4 + i][:, :], lat[:, kc * T + tc * 128: kc * T + tc * 128 + 128],
                                wbuf[:, kc * 1024 + n2 * 512: kc * 1024 + n2 * 512 + 512],
                                start=(kc == 0), stop=(kc == 3)),
                            wait=wl, inc=("pe" if kc == 3 else None))

            def evac_v(E, tc=tc, n2=n2, i=i):
                lo = tc * 1040 + n2 * 8 * 65
                dst = v_tm[:, lo:lo + 8 * 65].rearrange("q (h c) -> q h c", c=65)[:, :, 0:64]
                return E.tensor_copy(dst, ps[4 + i][:, :].rearrange("q (a c) -> q a c", a=8))
            vp_free[i] = k.op("vector", evac_v, wait=[("pe", t_mm)], inc="dve")
    t_vdone = k.tok("dve")
    t_rope = emit_rope(k, q_sb, 8, cc0_b, ss0_b, perm_b, ps[5], scr,
                       first_wait=[("dve", t_vdone)], ps_free=t_vdone)
    emit_rope(k, k_sb, 8, cc0_b, ss0_b, perm_b, ps[5], scr, ps_free=t_rope)
    k.flush()

    qk_free = [k.tok("act")] * 2
    e_wfree = [k.tok("pe")] * 2
    pv_free = [k.tok("dve")] * 2
    tiles = [(h, j) for h in range(NHEAD) for j in range(4)]
    exp_tok = {}

    def emit_qk_a(idx):
        h, j = tiles[idx]
        ch, base = h // 2, 64 * (h % 2)
        par = idx % 2
        nkt = min(4 * j + 4, 16)
        for kt0 in range(0, nkt, 2):
            pb2 = (kt0 // 2) % 2
            t_qk = None
            for kt in (kt0, kt0 + 1):
                t_qk = k.op("tensor",
                            lambda E, pb2=pb2, kt=kt, kt0=kt0, j=j, ch=ch, base=base: E.matmul(
                                pw[pb2][:, (kt - kt0) * 512:(kt - kt0) * 512 + 512],
                                k_sb[base:base + 64, ch * T + kt * 128: ch * T + kt * 128 + 128],
                                q_sb[base:base + 64, ch * T + j * 512: ch * T + j * 512 + 512],
                                start=True, stop=True),
                            wait=([("act", qk_free[pb2])] if kt == kt0 else []),
                            inc="pe")
            qk_free[pb2] = k.op("scalar",
                                lambda E, pb2=pb2, kt0=kt0, par=par: E.activation(
                                    ebuf[par][:, kt0 * 512: kt0 * 512 + 1024],
                                    pw[pb2][:, :], AF.Exp, scale=0.125),
                                wait=[("pe", t_qk),
                                      ("pe", e_wfree[par] if kt0 == 0 else 0)],
                                inc="act")
        exp_tok[idx] = qk_free[((nkt - 2) // 2) % 2]

    def emit_pv_a(idx):
        h, j = tiles[idx]
        par = idx % 2
        t_exp = exp_tok[idx]
        for qc in range(4):
            qcg = 4 * j + qc
            nk2 = min(2 * (qcg // 2) + 2, 16)
            pvp = qc % 2
            t_pv = None
            for kt in range(nk2):
                t_pv = k.op("tensor",
                            lambda E, pvp=pvp, kt=kt, qc=qc, h=h, par=par, nk2=nk2: E.matmul(
                                ps[4 + pvp][:, 0:65],
                                ebuf[par][:, kt * 512 + qc * 128: kt * 512 + qc * 128 + 128],
                                v_tm[:, kt * 1040 + 65 * h: kt * 1040 + 65 * h + 65],
                                start=(kt == 0), stop=(kt == nk2 - 1)),
                            wait=([("act", t_exp), ("dve", pv_free[pvp])] if kt == 0 else []),
                            inc=("pe" if kt == nk2 - 1 else None))
            if qc == 3:
                e_wfree[par] = t_pv
            t_rc = k.op("vector", lambda E, pvp=pvp: E.reciprocal(
                rcp[:, pvp:pvp + 1], ps[4 + pvp][:, 64:65]),
                wait=[("pe", t_pv)], inc="dve")
            pv_free[pvp] = k.op("vector",
                                lambda E, pvp=pvp, qcg=qcg, h=h: E.tensor_scalar_mul(
                                    attnout[:, qcg * 1024 + 64 * h: qcg * 1024 + 64 * h + 64],
                                    ps[4 + pvp][:, 0:64], rcp[:, pvp:pvp + 1]),
                                wait=[("dve", t_rc)], inc="dve")

    emit_qk_a(0)
    for i in range(len(tiles)):
        if i + 1 < len(tiles):
            emit_qk_a(i + 1)
        emit_pv_a(i)
    k.flush()

    t_wo = k.dma("gpsimd", wdst(wbuf, 8, 1024), wsrc(dr["wo"]))
    tr_free = [k.tok("dve")] * 2
    for tc in range(NTC):
        p = tc % 2
        t_tr = None
        for dc in range(NDC):
            t_tr = k.op("tensor",
                        lambda E, p=p, dc=dc, tc=tc: E.transpose(
                            pt[p][:, dc * 128:(dc + 1) * 128],
                            attnout[:, tc * 1024 + dc * 128: tc * 1024 + dc * 128 + 128],
                            ident_b[:, :]),
                        wait=([("dve", tr_free[p])] if dc == 0 else []),
                        inc=("pe" if dc == NDC - 1 else None))

        def evac_t(E, p=p, tc=tc):
            dst = ao_fm[:, :].rearrange("q (a t) -> q a t", a=8)[:, :, tc * 128:tc * 128 + 128]
            return E.tensor_copy(dst, pt[p][:, :].rearrange("q (a t) -> q a t", a=8))
        tr_free[p] = k.op("vector", evac_t, wait=[("pe", t_tr)], inc="dve")
    t_trd = k.tok("dve")
    radd_free = [0, 0]
    wo_ps_free = [k.tok("dve")] * 2
    for tc in range(NTC):
        t_mm = None
        for kc in range(NDC):
            for n2 in range(2):
                wl = []
                if kc == 0:
                    wl = [("dve", wo_ps_free[n2])]
                    if tc == 0:
                        wl += [("sg", t_wo), ("dve", t_trd)]
                t_mm = k.op("tensor",
                            lambda E, kc=kc, tc=tc, n2=n2: E.matmul(
                                ps[n2][:, :],
                                ao_fm[:, kc * T + tc * 128: kc * T + tc * 128 + 128],
                                wbuf[:, kc * 1024 + n2 * 512: kc * 1024 + n2 * 512 + 512],
                                start=(kc == 0), stop=(kc == NDC - 1)),
                            wait=wl, inc=("pe" if kc == NDC - 1 and n2 == 1 else None))

        def sc_fn(E, dst, psum, half):
            return E.tensor_copy(dst, psum[:, :])
        t_ev = emit_epilogue_chunk(k, b, tc, [ps[0], ps[1]], sc_fn,
                                   dr["x"].ap(), dr["r_out"].ap(), t_mm, radd_free)
        wo_ps_free[0] = t_ev
        wo_ps_free[1] = t_ev
    k.op("sync", lambda E: E.nop(),
         wait=[("st0", k.tok("st0")), ("st1", k.tok("st1"))])
    k.flush()



def emit_ffn0(nc, k, dr, ps, pt):
    a = Arena(nc)
    ident_b = a.alloc("B_ident", [128, 128], BF16)
    ident_f = a.alloc("B_identf", [128, 128], F32)
    s1c = a.alloc("B_s1", [128, 32], F32)
    s2c = a.alloc("B_s2", [128, 32], F32)
    hs1c = a.alloc("B_hs1", [128, 32], F32)
    hs2c = a.alloc("B_hs2", [128, 32], F32)
    s3c = a.alloc("B_s3", [128, 8], F32)
    silu_buf = a.alloc("B_silu", [128, 1024], F32)
    b = LNBufs(a, "B")
    y1 = a.alloc("B_y1", [128, 8 * 1024], BF16)
    h1h = a.alloc("B_h1h", [128, 32 * 1024], BF16)
    g1h = a.alloc("B_g1h", [128, 32 * 1024], BF16)
    wsl = [a.alloc(f"B_wsl{i}", [128, 8 * 512], BF16) for i in range(4)]
    hsl = [a.at(f"B_hsl{i}", [128, 32 * 256], BF16, a.offs["B_wsl0"] + i * 32 * 256 * 2)
           for i in range(2)]
    w3sl = [a.at(f"B_w3sl{i}", [128, 32 * 128], BF16,
                 a.offs["B_wsl0"] + i * 32 * 128 * 2) for i in range(2)]
    out_fm = a.at("B_outfm", [128, 8 * 1024], F32, a.offs["B_g1h"])


    k.dma("gpsimd", ident_b[:, :], dr["ident"].ap(), sem="scst")
    k.dma("gpsimd", ident_f[:, :], dr["ident"].ap(), sem="scst")
    k.dma("gpsimd", s1c[:, :], dr["s1"].ap(), sem="scst")
    k.dma("gpsimd", s2c[:, :], dr["s2"].ap(), sem="scst")
    k.dma("gpsimd", hs1c[:, :], dr["hs1"].ap(), sem="scst")
    k.dma("gpsimd", hs2c[:, :], dr["hs2"].ap(), sem="scst")
    k.dma("gpsimd", s3c[:, :], dr["s3fm"].ap(), sem="scst")
    t_cst = k.tok("scst")
    k.op("vector", lambda E: E.memset(b.eps[:, :], EPS))

    radd_free = [0, 0]
    for half in range(2):
        emit_ln_stream(k, b, pt, dr["r_in"].ap(), y1, None, ident_b,
                       row0=half * 1024, nchunks=8,
                       first_wait=[("scst", t_cst)],
                       rb_guard=[("st0", radd_free[0]), ("st1", radd_free[1])])
        k.flush()

        slab_free = {0: k.tok("pe"), 1: k.tok("pe")}
        t_w = {}
        for og in range(9):
            par = og % 2
            if og < 8:
                t_w[(og, 1)] = k.dma(
                    "gpsimd", wdst(wsl[par], 8, 512),
                    dr["w1t"].ap()[:, og * 512:(og + 1) * 512]
                    .rearrange("(c p) m -> p c m", p=128),
                    wait=[("pe", slab_free[par])], sem=f"wA{par}")
                t_w[(og, 2)] = k.dma(
                    "gpsimd", wdst(wsl[2 + par], 8, 512),
                    dr["w2t"].ap()[:, og * 512:(og + 1) * 512]
                    .rearrange("(c p) m -> p c m", p=128),
                    sem=f"wB{par}")
            if og == 0:
                continue
            ogc = og - 1
            parc = ogc % 2
            psfree = [k.tok("act"), k.tok("act"), k.tok("dve"), k.tok("dve")]
            silu_free = [k.tok("dve")] * 2
            for oc in range(4):
                o = ogc * 4 + oc
                t_a = [None, None]
                t_b = [None, None]
                for kc in range(8):
                    for tt in range(2):
                        wl = []
                        if kc == 0 and tt == 0:
                            wl = [("act", psfree[0]), ("act", psfree[1]),
                                  (f"wA{parc}", t_w[(ogc, 1)])]
                        t_a[tt] = k.op("tensor",
                                       lambda E, tt=tt, kc=kc, oc=oc, parc=parc:
                                       E.matmul(ps[tt][:, :],
                                                wsl[parc][:, (kc * 4 + oc) * 128:(kc * 4 + oc) * 128 + 128],
                                                y1[:, kc * 1024 + tt * 512: kc * 1024 + tt * 512 + 512],
                                                start=(kc == 0), stop=(kc == 7)),
                                       wait=wl, inc=("pe" if kc == 7 else None))
                for kc in range(8):
                    for tt in range(2):
                        wl = []
                        if kc == 0 and tt == 0:
                            wl = [("dve", psfree[2]), ("dve", psfree[3]),
                                  (f"wB{parc}", t_w[(ogc, 2)])]
                        t_b[tt] = k.op("tensor",
                                       lambda E, tt=tt, kc=kc, oc=oc, parc=parc:
                                       E.matmul(ps[2 + tt][:, :],
                                                wsl[2 + parc][:, (kc * 4 + oc) * 128:(kc * 4 + oc) * 128 + 128],
                                                y1[:, kc * 1024 + tt * 512: kc * 1024 + tt * 512 + 512],
                                                start=(kc == 0), stop=(kc == 7)),
                                       wait=wl, inc=("pe" if kc == 7 else None))
                for tt in range(2):
                    psfree[tt] = k.op("scalar",
                                      lambda E, tt=tt, o=o: E.activation(
                                          silu_buf[:, tt * 512:(tt + 1) * 512],
                                          ps[tt][:, :], AF.Silu, scale=s1c[:, o:o + 1]),
                                      wait=[("pe", t_a[tt]), ("dve", silu_free[tt])],
                                      inc="act", drain=True)
                    silu_free[tt] = k.op("vector",
                                         lambda E, tt=tt, o=o: E.scalar_tensor_tensor(
                                             h1h[:, o * 1024 + tt * 512: o * 1024 + tt * 512 + 512],
                                             ps[2 + tt][:, :], s2c[:, o:o + 1],
                                             silu_buf[:, tt * 512:(tt + 1) * 512],
                                             ALU.mult, ALU.mult),
                                         wait=[("pe", t_b[tt]), ("act", psfree[tt])],
                                         inc="dve")
                    psfree[2 + tt] = silu_free[tt]
            slab_free[parc] = k.tok("pe")
        k.flush()

        for stage in range(2):
            src = h1h if stage == 0 else g1h
            dsts = g1h if stage == 0 else h1h
            wname = "h1t" if stage == 0 else "h2t"
            hsc = hs1c if stage == 0 else hs2c
            slab_free = {0: k.tok("pe"), 1: k.tok("pe")}
            t_w = {}
            for og in range(17):
                par = og % 2
                if og < 16:
                    t_w[og] = k.dma(
                        "gpsimd", wdst(hsl[par], 32, 256),
                        dr[wname].ap()[:, og * 256:(og + 1) * 256]
                        .rearrange("(c p) m -> p c m", p=128),
                        wait=[("pe", slab_free[par])], sem=f"wh{par}")
                if og == 0:
                    continue
                ogc = og - 1
                parc = ogc % 2
                psfree = [k.tok("act")] * 4
                t_mm4 = [None] * 4
                for kc in range(32):
                    for oc2 in range(2):
                        for tt in range(2):
                            pi = oc2 * 2 + tt
                            wl = []
                            if kc == 0 and pi == 0:
                                wl = [("act", t) for t in psfree] + [(f"wh{parc}", t_w[ogc])]
                            t_mm4[pi] = k.op("tensor",
                                             lambda E, pi=pi, kc=kc, oc2=oc2, tt=tt, parc=parc, src=src:
                                             E.matmul(ps[pi][:, :],
                                                      hsl[parc][:, (kc * 2 + oc2) * 128:(kc * 2 + oc2) * 128 + 128],
                                                      src[:, kc * 1024 + tt * 512: kc * 1024 + tt * 512 + 512],
                                                      start=(kc == 0), stop=(kc == 31)),
                                             wait=wl, inc=("pe" if kc == 31 else None))
                for oc2 in range(2):
                    o = ogc * 2 + oc2
                    for tt in range(2):
                        pi = oc2 * 2 + tt
                        psfree[pi] = k.op("scalar",
                                          lambda E, pi=pi, o=o, tt=tt, dsts=dsts, hsc=hsc:
                                          E.activation(
                                              dsts[:, o * 1024 + tt * 512: o * 1024 + tt * 512 + 512],
                                              ps[pi][:, :], AF.Gelu, scale=hsc[:, o:o + 1]),
                                          wait=[("pe", t_mm4[pi])], inc="act", drain=(pi == 0))
                slab_free[parc] = k.tok("pe")
            k.flush()

        g2h = h1h
        slab_free = {0: k.tok("pe"), 1: k.tok("pe")}
        t_w = {}
        for oc in range(9):
            par = oc % 2
            if oc < 8:
                t_w[oc] = k.dma(
                    "gpsimd", wdst(w3sl[par], 32, 128),
                    dr["w3t"].ap()[:, oc * 128:(oc + 1) * 128]
                    .rearrange("(c p) m -> p c m", p=128),
                    wait=[("pe", slab_free[par])], sem=f"w3{par}")
            if oc == 0:
                continue
            occ = oc - 1
            parc = occ % 2
            psfree = [k.tok("dve")] * 2
            t_mm2 = [None, None]
            for kc in range(32):
                for tt in range(2):
                    wl = []
                    if kc == 0 and tt == 0:
                        wl = [("dve", psfree[0]), ("dve", psfree[1]),
                              (f"w3{parc}", t_w[occ])]
                    t_mm2[tt] = k.op("tensor",
                                     lambda E, tt=tt, kc=kc, parc=parc:
                                     E.matmul(ps[tt][:, :],
                                              w3sl[parc][:, kc * 128:kc * 128 + 128],
                                              g2h[:, kc * 1024 + tt * 512: kc * 1024 + tt * 512 + 512],
                                              start=(kc == 0), stop=(kc == 31)),
                                     wait=wl, inc=("pe" if kc == 31 else None))
            for tt in range(2):
                psfree[tt] = k.op("vector",
                                  lambda E, tt=tt, occ=occ: E.tensor_scalar_mul(
                                      out_fm[:, occ * 1024 + tt * 512: occ * 1024 + tt * 512 + 512],
                                      ps[tt][:, :], s3c[:, occ:occ + 1]),
                                  wait=[("pe", t_mm2[tt])], inc="dve")
            slab_free[parc] = k.tok("pe")
        t_w3d = k.tok("dve")
        k.flush()

        ps_free_t = k.tok("dve")
        for tcl in range(8):
            tc = half * 8 + tcl
            t_tr = None
            for dc in range(NDC):
                wl = []
                if dc == 0:
                    wl = [("dve", ps_free_t)]
                    if tcl == 0:
                        wl.append(("dve", t_w3d))
                t_tr = k.op("tensor",
                            lambda E, dc=dc, tcl=tcl: E.transpose(
                                ps[4 + dc // 4][:, (dc % 4) * 128:(dc % 4) * 128 + 128],
                                out_fm[:, dc * 1024 + tcl * 128: dc * 1024 + tcl * 128 + 128],
                                ident_f[:, :]),
                            wait=wl, inc=("pe" if dc == 7 else None))

            def sc_fn(E, dst, psum, hh):
                return E.tensor_copy(dst, psum[:, :])
            ps_free_t = emit_epilogue_chunk(k, b, tc, [ps[4], ps[5]], sc_fn,
                                            dr["r_in"].ap(), dr["r_out"].ap(),
                                            t_tr, radd_free)
        k.flush()
    k.op("sync", lambda E: E.nop(),
         wait=[("st0", k.tok("st0")), ("st1", k.tok("st1"))])
    k.flush()


def emit_attn1(nc, k, dr, ps, pt, pw):
    a = Arena(nc)
    ident_b = a.alloc("C_ident", [128, 128], BF16, align=32)
    cc1_b = a.alloc("C_cc1", [128, T], BF16, align=32)
    ss1_b = a.alloc("C_ss1", [128, T], BF16, align=32)
    perm_b = a.alloc("C_perm", [128, 128], BF16, align=32)
    tril_b = a.alloc("C_tril", [128, 128], BF16, align=32)
    v_tm = a.alloc("C_v", [128, NTC * 1040], BF16, align=32)
    cq = a.alloc("C_cq", [128, 4 * T], BF16, align=32)
    ckv = a.alloc("C_ckv", [128, 4 * T], BF16, align=32)
    qcp = [a.alloc(f"C_qcp{i}", [128, T], BF16, align=32) for i in range(2)]
    kcp = [a.alloc(f"C_kcp{i}", [128, T], BF16, align=32) for i in range(2)]
    qrp = [a.alloc(f"C_qrp{i}", [128, T], BF16, align=32) for i in range(2)]
    wuq_sb = a.alloc("C_wuq", [128, 4 * 1024], BF16, align=32)
    wuk_sb = a.alloc("C_wuk", [128, 4 * 1024], BF16, align=32)
    wqr_sb = a.alloc("C_wqr", [128, 4 * 1024], BF16, align=32)
    kr_pad = a.alloc("C_kr", [128, T], BF16, align=32)
    attnout = a.alloc("C_ao", [128, NTC * 1024], BF16, align=32)
    ov = a.off
    y2_fm = a.at("C_y2fm", [128, 8 * T], BF16, ov)
    ebuf = [a.at(f"C_e{i}", [128, 16 * 512], BF16, ov + i * 16 * 512 * 2) for i in range(2)]
    ao_fm = a.at("C_aofm", [128, 8 * T], BF16, ov)
    a.off = ov + 8 * T * 2
    wbuf = a.at("C_wbuf", [128, 8 * 512], BF16, a.offs["C_kcp0"])
    wo_sb = a.at("C_wo", [128, 8 * 1024], BF16, a.offs["C_qcp0"])
    scr = a.alloc("C_scr", [128, 1024], BF16, align=32)
    rcp = a.alloc("C_rcp", [128, 2], F32, align=32)
    emask = [a.alloc(f"C_em{i}", [128, 128], BF16, align=32) for i in range(2)]
    b = LNBufs(a, "C")


    for nm, dst in [("ident", ident_b), ("cc1", cc1_b), ("ss1", ss1_b),
                    ("perm1m", perm_b), ("tril", tril_b)]:
        k.dma("gpsimd", dst[:, :], dr[nm].ap(), sem="scst")
    t_cst = k.tok("scst")
    t_ones = k.op("vector", lambda E: E.memset(
        v_tm[:, :].rearrange("q (h c) -> q h c", c=65)[:, :, 64:65], 1.0), inc="dve")
    k.op("vector", lambda E: E.memset(b.eps[:, :], EPS))
    emit_ln_stream(k, b, pt, dr["r_in"].ap(), y2_fm, None, ident_b,
                   first_wait=[("scst", t_cst)])
    k.flush()

    t_wdq = k.dma("gpsimd", wdst(wbuf, 8, 512), wsrc(dr["wdq"]), sem="sg")
    emit_proj(k, wbuf, 8, 4, y2_fm, T, cq, T, ps[4:6], 4, 512,
              extra_wait=[("sg", t_wdq)])
    t_cq = k.tok("dve")
    t_wdkv = k.dma("gpsimd", wdst(wbuf, 8, 512), wsrc(dr["wdkv"]),
                   wait=[("dve", t_cq)], sem="sg")
    emit_proj(k, wbuf, 8, 4, y2_fm, T, ckv, T, ps[4:6], 4, 512,
              extra_wait=[("sg", t_wdkv)])
    t_ckv = k.tok("dve")
    t_wkr = k.dma("gpsimd", wdst(wbuf, 8, 128), wsrc(dr["wkrp"]),
                  wait=[("dve", t_ckv)], sem="sg")
    emit_proj(k, wbuf, 8, 1, y2_fm, T, kr_pad, T, ps[4:6], 4, 512,
              extra_wait=[("sg", t_wkr)])
    t_kr = k.tok("dve")
    t_krr = emit_rope(k, kr_pad, 1, cc1_b, ss1_b, perm_b, ps[4], scr,
                      first_wait=[("dve", t_kr)], ps_free=t_kr)
    t_dup = k.dma("sync", kr_pad[64:96, :], kr_pad[0:32, :],
                  wait=[("dve", t_krr)], sem="kr")
    t_wuv = k.dma("gpsimd", wdst(wbuf, 4, 1024), wsrc(dr["wuv"]),
                  wait=[("dve", t_kr)], sem="sg")
    vp_free = [k.tok("dve")] * 2
    for tc in range(NTC):
        for n2 in range(2):
            i = n2
            t_mm = None
            for kc in range(4):
                wl = []
                if kc == 0:
                    wl = [("dve", vp_free[i])]
                    if tc == 0 and n2 == 0:
                        wl += [("sg", t_wuv), ("dve", t_ones), ("dve", t_ckv)]
                t_mm = k.op("tensor",
                            lambda E, i=i, kc=kc, tc=tc, n2=n2: E.matmul(
                                ps[i][:, :], ckv[:, kc * T + tc * 128: kc * T + tc * 128 + 128],
                                wbuf[:, kc * 1024 + n2 * 512: kc * 1024 + n2 * 512 + 512],
                                start=(kc == 0), stop=(kc == 3)),
                            wait=wl, inc=("pe" if kc == 3 else None))

            def evac_v(E, tc=tc, n2=n2, i=i):
                lo = tc * 1040 + n2 * 8 * 65
                dst = v_tm[:, lo:lo + 8 * 65].rearrange("q (h c) -> q h c", c=65)[:, :, 0:64]
                return E.tensor_copy(dst, ps[i][:, :].rearrange("q (a c) -> q a c", a=8))
            vp_free[i] = k.op("vector", evac_v, wait=[("pe", t_mm)], inc="dve")
    k.dma("gpsimd", wdst(wuq_sb, 4, 1024), wsrc(dr["wuq"]), sem="sgr")
    k.dma("gpsimd", wdst(wuk_sb, 4, 1024), wsrc(dr["wuk"]), sem="sgr")
    t_wres = k.dma("gpsimd", wdst(wqr_sb, 4, 1024), wsrc(dr["wqrp"]), sem="sgr")
    k.flush()

    SCALE1 = 1.0 / float(np.sqrt(96.0))
    head_done = [k.tok("pe")] * 2
    proj_done = [k.tok("dve")] * 2
    qk_free = [k.tok("act")] * 2
    e_wfree = [k.tok("pe")] * 2
    pv_free = [k.tok("dve")] * 2
    em_free = [k.tok("pe")] * 2
    ps45_free = k.tok("dve")

    def pair_proj(c):
        nonlocal ps45_free
        pb = c % 2
        toks = [ps45_free, ps45_free]
        pvw = [pt[0][:, :].bitcast(F32), pt[1][:, :].bitcast(F32)]
        for which, wsb, src, dst in [(0, wuq_sb, cq, qcp[pb]), (1, wuk_sb, ckv, kcp[pb]),
                                     (2, wqr_sb, cq, qrp[pb])]:
            for tt in range(4):
                i = tt % 2
                t_mm = None
                for kc in range(4):
                    wl = []
                    if kc == 0:
                        wl = [("dve", toks[i]), ("pe", head_done[pb])]
                        if c == 0 and which == 0 and tt == 0:
                            wl.append(("sgr", t_wres))
                    t_mm = k.op("tensor",
                                lambda E, i=i, kc=kc, tt=tt, c=c, wsb=wsb, src=src, pvw=pvw:
                                E.matmul(pvw[i],
                                         wsb[:, kc * 1024 + c * 128: kc * 1024 + c * 128 + 128],
                                         src[:, kc * T + tt * 512: kc * T + tt * 512 + 512],
                                         start=(kc == 0), stop=(kc == 3)),
                                wait=wl, inc=("pe" if kc == 3 else None))
                toks[i] = k.op("vector",
                               lambda E, tt=tt, dst=dst, i=i, pvw=pvw: E.tensor_copy(
                                   dst[:, tt * 512: tt * 512 + 512], pvw[i]),
                               wait=[("pe", t_mm)], inc="dve")
        t = emit_rope(k, qrp[pb], 1, cc1_b, ss1_b, perm_b, pt[0][:, :].bitcast(F32), scr,
                      first_wait=[("dve", max(toks))], ps_free=max(toks))
        ps45_free = t
        return t

    proj_done[0] = pair_proj(0)
    SC = SCALE1
    exp_tok_c = {}

    def emit_qk_c(c, h2, j):
        pb = c % 2
        base = 64 * h2
        par = (8 * c + 4 * h2 + j) % 2
        nkt = min(4 * j + 4, 16)
        for kt0 in range(0, nkt, 2):
            pb2 = (kt0 // 2) % 2
            t_qk = None
            for kt in (kt0, kt0 + 1):
                wl1 = []
                if kt == kt0:
                    wl1 = [("act", qk_free[pb2])]
                    if h2 == 0 and j == 0 and kt0 == 0:
                        wl1 += [("dve", proj_done[pb]), ("kr", t_dup)]
                k.op("tensor",
                     lambda E, pb2=pb2, kt=kt, kt0=kt0, j=j, base=base, pb=pb: E.matmul(
                         pw[pb2][:, (kt - kt0) * 512:(kt - kt0) * 512 + 512],
                         kcp[pb][base:base + 64, kt * 128: kt * 128 + 128],
                         qcp[pb][base:base + 64, j * 512: j * 512 + 512],
                         start=True, stop=False),
                     wait=wl1)
                t_qk = k.op("tensor",
                            lambda E, pb2=pb2, kt=kt, kt0=kt0, j=j, base=base, pb=pb: E.matmul(
                                pw[pb2][:, (kt - kt0) * 512:(kt - kt0) * 512 + 512],
                                kr_pad[base:base + 32, kt * 128: kt * 128 + 128],
                                qrp[pb][base:base + 32, j * 512: j * 512 + 512],
                                start=False, stop=True),
                            inc="pe")
            qk_free[pb2] = k.op("scalar",
                                lambda E, pb2=pb2, kt0=kt0, par=par: E.activation(
                                    ebuf[par][:, kt0 * 512: kt0 * 512 + 1024],
                                    pw[pb2][:, :], AF.Exp, scale=SC),
                                wait=[("pe", t_qk),
                                      ("pe", e_wfree[par] if kt0 == 0 else 0)],
                                inc="act")
        exp_tok_c[(c, h2, j)] = qk_free[((nkt - 2) // 2) % 2]

    def emit_pv_c(c, h2, j):
        h = 2 * c + h2
        par = (8 * c + 4 * h2 + j) % 2
        t_exp = exp_tok_c[(c, h2, j)]
        last = None
        for qc in range(4):
            qcg = 4 * j + qc
            nk2 = qcg + 1
            pvp = qc % 2
            emp = qc % 2
            t_em = k.op("vector",
                        lambda E, par=par, qcg=qcg, qc=qc, emp=emp: E.tensor_tensor(
                            emask[emp][:, :],
                            ebuf[par][:, qcg * 512 + qc * 128: qcg * 512 + qc * 128 + 128],
                            tril_b[:, :], ALU.mult),
                        wait=[("act", t_exp), ("pe", em_free[emp])],
                        inc="dve")
            t_pv = None
            for kt in range(nk2):
                lhs = (emask[emp][:, :] if kt == qcg else
                       ebuf[par][:, kt * 512 + qc * 128: kt * 512 + qc * 128 + 128])
                wl = []
                if kt == 0:
                    wl = [("act", t_exp), ("dve", pv_free[pvp]), ("dve", t_em)]
                t_pv = k.op("tensor",
                            lambda E, pvp=pvp, kt=kt, h=h, lhs=lhs, nk2=nk2: E.matmul(
                                ps[4 + pvp][:, 0:65], lhs,
                                v_tm[:, kt * 1040 + 65 * h: kt * 1040 + 65 * h + 65],
                                start=(kt == 0), stop=(kt == nk2 - 1)),
                            wait=wl, inc=("pe" if kt == nk2 - 1 else None))
            em_free[emp] = t_pv
            last = t_pv
            if qc == 3:
                e_wfree[par] = t_pv
            t_rc = k.op("vector", lambda E, pvp=pvp: E.reciprocal(
                rcp[:, pvp:pvp + 1], ps[4 + pvp][:, 64:65]),
                wait=[("pe", t_pv)], inc="dve")
            pv_free[pvp] = k.op("vector",
                                lambda E, pvp=pvp, qcg=qcg, h=h: E.tensor_scalar_mul(
                                    attnout[:, qcg * 1024 + 64 * h: qcg * 1024 + 64 * h + 64],
                                    ps[4 + pvp][:, 0:64], rcp[:, pvp:pvp + 1]),
                                wait=[("dve", t_rc)], inc="dve")
        return last

    pending = None
    for c in range(8):
        if c < 7:
            proj_done[(c + 1) % 2] = pair_proj(c + 1)
        for h2 in range(2):
            for j in range(4):
                emit_qk_c(c, h2, j)
                if pending is not None:
                    lp = emit_pv_c(*pending)
                    head_done[pending[0] % 2] = lp
                pending = (c, h2, j)
    lp = emit_pv_c(*pending)
    head_done[pending[0] % 2] = lp
    k.flush()

    t_wo = k.dma("gpsimd", wdst(wo_sb, 8, 1024), wsrc(dr["wot"]), sem="sg")
    tr_free = [k.tok("dve")] * 2
    for tc in range(NTC):
        p = tc % 2
        t_tr = None
        for dc in range(NDC):
            t_tr = k.op("tensor",
                        lambda E, p=p, dc=dc, tc=tc: E.transpose(
                            pt[p][:, dc * 128:(dc + 1) * 128],
                            attnout[:, tc * 1024 + dc * 128: tc * 1024 + dc * 128 + 128],
                            ident_b[:, :]),
                        wait=([("dve", tr_free[p])] if dc == 0 else []),
                        inc=("pe" if dc == NDC - 1 else None))

        def evac_t(E, p=p, tc=tc):
            dst = ao_fm[:, :].rearrange("q (a t) -> q a t", a=8)[:, :, tc * 128:tc * 128 + 128]
            return E.tensor_copy(dst, pt[p][:, :].rearrange("q (a t) -> q a t", a=8))
        tr_free[p] = k.op("vector", evac_t, wait=[("pe", t_tr)], inc="dve")
    t_trd = k.tok("dve")
    radd_free = [0, 0]
    wo_ps_free = [k.tok("dve")] * 2
    for tc in range(NTC):
        t_mm = None
        for kc in range(NDC):
            for n2 in range(2):
                wl = []
                if kc == 0:
                    wl = [("dve", wo_ps_free[n2])]
                    if tc == 0:
                        wl += [("sg", t_wo), ("dve", t_trd)]
                t_mm = k.op("tensor",
                            lambda E, kc=kc, tc=tc, n2=n2: E.matmul(
                                ps[n2][:, :],
                                ao_fm[:, kc * T + tc * 128: kc * T + tc * 128 + 128],
                                wo_sb[:, kc * 1024 + n2 * 512: kc * 1024 + n2 * 512 + 512],
                                start=(kc == 0), stop=(kc == NDC - 1)),
                            wait=wl, inc=("pe" if kc == NDC - 1 and n2 == 1 else None))

        def sc_fn(E, dst, psum, half):
            return E.tensor_copy(dst, psum[:, :])
        t_ev = emit_epilogue_chunk(k, b, tc, [ps[0], ps[1]], sc_fn,
                                   dr["r_in"].ap(), dr["r_out"].ap(), t_mm, radd_free)
        wo_ps_free[0] = t_ev
        wo_ps_free[1] = t_ev
    k.op("sync", lambda E: E.nop(),
         wait=[("st0", k.tok("st0")), ("st1", k.tok("st1"))])
    k.flush()


def emit_ffn1(nc, k, dr, ps, pt):
    a = Arena(nc)
    ident_b = a.alloc("D_ident", [128, 128], BF16)
    sel4_b = a.alloc("D_sel4", [32, 128], BF16)
    ones_b = a.alloc("D_ones", [1, 128], BF16)
    phib_b = a.alloc("D_phib", [1, 2048], BF16)
    ebfm = a.alloc("D_ebfm", [128, 64], F32)
    b = LNBufs(a, "D")
    sp = a.alloc("D_sp", [128, 16 * 64], F32)
    sn = a.alloc("D_sn", [128, 16 * 32], F32)
    snb = a.alloc("D_snb", [128, 512], BF16)
    rsp = a.alloc("D_rsp", [128, 1024], BF16)
    rsp_f = a.alloc("D_rspf", [128, 1024], F32)
    rsnt = a.alloc("D_rsnt", [32, T], BF16)
    rsnt_f = a.alloc("D_rsntf", [32, T], F32)
    rep = a.alloc("D_rep", [128, T], BF16)
    dstage = [a.alloc(f"D_dst{i}", [128, 128], BF16) for i in range(2)]
    ect = a.alloc("D_ect", [128, 16 * T], BF16)
    y3_fm = a.alloc("D_y3f", [128, 8 * T], BF16)
    y3_tm = a.alloc("D_y3t", [128, 16 * 1024], BF16)
    phi_sb = a.alloc("D_phi", [128, 8 * 2048], BF16)
    esl = a.at("D_esl", [128, 8 * 1024], BF16, a.offs["D_y3f"])
    xs2 = a.at("D_xs2", [128, 2 * 1024], F32, a.offs["D_y3f"] + 16384)
    xtanh = a.at("D_xtanh", [128, 2 * 1024], BF16, a.offs["D_y3f"] + 16384 + 8192)
    x_fm = a.at("D_xfm", [128, 8 * 256], BF16, a.offs["D_y3f"] + 16384 + 8192 + 4096)
    ytt_fm = a.at("D_ytt", [128, 8 * 256], BF16, a.offs["D_phi"])
    ytst = a.at("D_ytst", [128, 1024], BF16, a.offs["D_phi"] + 4096)
    yt_np = a.at("D_ytnp", [128, 16 * 1024], BF16, a.offs["D_y3t"])
    ct = ect


    xp_d = nc.dram_tensor("D_xp", [2048, 1024], F32)
    xs_d = nc.dram_tensor("D_xs", [256, 1024], F32)
    ytp_d = nc.dram_tensor("D_ytp", [256, 1024], BF16)
    ytf_d = nc.dram_tensor("D_ytf", [2048, 1024], BF16)

    for nm, dst in [("ident", ident_b), ("sel4", sel4_b), ("ones1", ones_b),
                    ("phib", phib_b), ("ebfm", ebfm)]:
        k.dma("gpsimd", dst[:, :], dr[nm].ap(), sem="scst")
    t_cst = k.tok("scst")
    k.op("vector", lambda E: E.memset(b.eps[:, :], EPS))
    emit_ln_stream(k, b, pt, dr["r_in"].ap(), y3_fm, y3_tm, ident_b,
                   first_wait=[("scst", t_cst)])
    k.flush()

    t_phi = k.dma("gpsimd", wdst(phi_sb, 8, 2048), wsrc(dr["phit"]), sem="sg")
    ps_free = [k.tok("act")] * 4
    sum_free = k.tok("dve")
    for tc in range(NTC):
        t_e = [None] * 4
        for n in range(4):
            wl = [("act", ps_free[n])]
            if tc == 0 and n == 0:
                wl.append(("sg", t_phi))
            k.op("tensor",
                 lambda E, n=n: E.matmul(
                     ps[n][:, :], ones_b[0:1, :], phib_b[0:1, n * 512:(n + 1) * 512],
                     start=True, stop=False),
                 wait=wl)
            t_mm = None
            for kc in range(NDC):
                t_mm = k.op("tensor",
                            lambda E, n=n, kc=kc, tc=tc: E.matmul(
                                ps[n][:, :],
                                y3_fm[:, kc * T + tc * 128: kc * T + tc * 128 + 128],
                                phi_sb[:, kc * 2048 + n * 512: kc * 2048 + n * 512 + 512],
                                start=False, stop=(kc == NDC - 1)),
                            inc=("pe" if kc == NDC - 1 else None))
            t_e[n] = k.op("scalar",
                          lambda E, n=n, tc=tc: E.activation(
                              ect[:, tc * 2048 + n * 512: tc * 2048 + n * 512 + 512],
                              ps[n][:, :], AF.Exp),
                          wait=[("pe", t_mm)], inc="act")
            ps_free[n] = t_e[n]
        ev = ect[:, tc * 2048:(tc + 1) * 2048]
        k.op("vector",
             lambda E, ev=ev, tc=tc: E.reduce_sum(
                 sp[:, tc * 64:(tc + 1) * 64],
                 ev.rearrange("q (n p2) -> q n p2", p2=32), axis=AX.X),
             wait=[("act", t_e[3]), ("dve", sum_free)])
        sum_free = k.op("vector",
                        lambda E, ev=ev, tc=tc: E.reduce_sum(
                            sn[:, tc * 32:(tc + 1) * 32],
                            ev.rearrange("q (n p2) -> q p2 n", p2=32), axis=AX.X),
                        inc="dve")
    k.op("vector", lambda E: E.reciprocal(rsp_f[:, :], sp[:, :]), drain=True)
    t_rsp = k.op("vector", lambda E: E.tensor_copy(rsp[:, :], rsp_f[:, :]),
                 inc="dve", drain=True)
    t_snb = k.op("vector", lambda E: E.tensor_copy(snb[:, :], sn[:, :]),
                 inc="dve", drain=True)
    t_tr = None
    for tc in range(NTC):
        t_tr = k.op("tensor",
                    lambda E, tc=tc: E.transpose(
                        pt[tc // 8][0:32, (tc % 8) * 128:(tc % 8) * 128 + 128],
                        snb[:, tc * 32:(tc + 1) * 32], ident_b[:, :]),
                    wait=([("dve", t_snb)] if tc == 0 else []),
                    inc=("pe" if tc == NTC - 1 else None))
    k.op("vector", lambda E: E.tensor_copy(rsnt[0:32, 0:1024], pt[0][0:32, :]),
         wait=[("pe", t_tr)], drain=True)
    k.op("vector", lambda E: E.tensor_copy(rsnt[0:32, 1024:2048], pt[1][0:32, :]),
         drain=True)
    k.op("vector", lambda E: E.tensor_copy(rsnt_f[0:32, :], rsnt[0:32, :]), drain=True)
    k.op("vector", lambda E: E.reciprocal(rsnt_f[0:32, :], rsnt_f[0:32, :]), drain=True)
    t_rsnt = k.op("vector", lambda E: E.tensor_copy(rsnt[0:32, :], rsnt_f[0:32, :]),
                  inc="dve", drain=True)
    t_rep = t_rsnt
    for n in range(4):
        t_sel = k.op("tensor",
                     lambda E, n=n: E.matmul(ps[4][:, :], sel4_b[0:32, :],
                                             rsnt[0:32, n * 512:(n + 1) * 512],
                                             start=True, stop=True),
                     wait=[("dve", t_rep)], inc="pe")
        t_rep = k.op("vector",
                     lambda E, n=n: E.tensor_copy(rep[:, n * 512:(n + 1) * 512],
                                                  ps[4][:, :]),
                     wait=[("pe", t_sel)], inc="dve", drain=True)
    k.flush()

    xstore_free = [0, 0]
    dpar_free = [k.tok("pe")] * 2
    for npc in range(NTC):
        pxp = npc % 2
        pa, pb2 = 2 + 2 * pxp, 3 + 2 * pxp
        t_mm = None
        for tc in range(NTC):
            dpar = tc % 2
            t_dst = k.op("vector",
                         lambda E, tc=tc, npc=npc, dpar=dpar: E.tensor_tensor(
                             dstage[dpar][:, :].rearrange("q (n p2) -> q n p2", p2=32),
                             ect[:, tc * 2048 + npc * 128: tc * 2048 + npc * 128 + 128]
                             .rearrange("q (n p2) -> q n p2", p2=32),
                             rsp[:, tc * 64 + npc * 4: tc * 64 + npc * 4 + 4][:, :, None]
                             .broadcast_to([128, 4, 32]),
                             ALU.mult),
                         wait=[("pe", dpar_free[dpar])], inc="dve")
            for n2 in range(2):
                wl = [("dve", t_dst)]
                if tc == 0 and n2 == 0:
                    wl.append((f"xp{pxp}", xstore_free[pxp]))
                t_mm = k.op("tensor",
                            lambda E, tc=tc, n2=n2, pp=(pa if n2 == 0 else pb2), dpar=dpar:
                            E.matmul(
                                ps[pp][:, :], dstage[dpar][:, :],
                                y3_tm[:, tc * 1024 + n2 * 512: tc * 1024 + n2 * 512 + 512],
                                start=(tc == 0), stop=(tc == NTC - 1)),
                            wait=wl, inc=("pe" if n2 == 1 else None))
            dpar_free[dpar] = t_mm
        k.op("vector", lambda E, pa=pa, pxp=pxp: E.tensor_copy(
            b.rb[pxp][:, 0:512], ps[pa][:, :]),
            wait=[("pe", t_mm)], drain=True)
        t_x = k.op("vector", lambda E, pb2=pb2, pxp=pxp: E.tensor_copy(
            b.rb[pxp][:, 512:1024], ps[pb2][:, :]),
            inc="dve", drain=True)
        xstore_free[pxp] = k.dma("sync", xp_d.ap()[npc * 128:(npc + 1) * 128, :],
                                 b.rb[pxp][:, :], wait=[("dve", t_x)], sem=f"xp{pxp}")
    k.op("gpsimd", lambda E: E.collective_compute(
        "ReduceScatter", ALU.add, replica_groups=[list(range(NCORES))],
        ins=[xp_d.ap().opt()], outs=[xs_d.ap().opt()]),
        wait=[("xp0", k.tok("xp0")), ("xp1", k.tok("xp1"))], inc="cc")
    ctp_free = [k.tok("dve")] * 2
    pairs = [(i, j) for i in range(NTC) for j in range(i, NTC)]
    for idx, (i, j) in enumerate(pairs):
        pq = idx % 2
        t1 = k.op("tensor",
                  lambda E, i=i, j=j, pq=pq: E.transpose(
                      pt[pq][:, 0:128],
                      ect[:, i * 2048 + j * 128: i * 2048 + j * 128 + 128],
                      ident_b[:, :]),
                  wait=[("dve", ctp_free[pq])],
                  inc=("pe" if i == j else None))
        if i != j:
            t1 = k.op("tensor",
                      lambda E, i=i, j=j, pq=pq: E.transpose(
                          pt[pq][:, 128:256],
                          ect[:, j * 2048 + i * 128: j * 2048 + i * 128 + 128],
                          ident_b[:, :]),
                      inc="pe")
        tlast = k.op("vector",
                     lambda E, i=i, j=j, pq=pq: E.tensor_tensor(
                         ect[:, j * 2048 + i * 128: j * 2048 + i * 128 + 128],
                         pt[pq][:, 0:128], rep[:, i * 128: i * 128 + 128], ALU.mult),
                     wait=[("pe", t1)], inc=("dve" if i == j else None))
        if i != j:
            tlast = k.op("vector",
                         lambda E, i=i, j=j, pq=pq: E.tensor_tensor(
                             ect[:, i * 2048 + j * 128: i * 2048 + j * 128 + 128],
                             pt[pq][:, 128:256], rep[:, j * 128: j * 128 + 128], ALU.mult),
                         inc="dve")
        ctp_free[pq] = tlast
    t_ct = k.tok("dve")
    k.flush()

    t_xs = k.dma("sync", xs2[:, :].rearrange("q (c d) -> q c d", c=2),
                 xs_d.ap().rearrange("(c q) d -> q c d", q=128),
                 wait=[("cc", 1)], sem="xs")
    k.op("vector", lambda E: E.tensor_scalar(xs2[:, :], xs2[:, :], -33000.0, 65000.0,
                                             ALU.max, ALU.min),
         wait=[(f"xs", t_xs)])
    t_xfm = k.tok("dve")
    for c2 in range(2):
        sl = xs2[:, c2 * 1024:(c2 + 1) * 1024]
        k.op("vector", lambda E, sl=sl: E.bn_stats(b.st[:, 0:6], sl[:, 0:512]),
             drain=True)
        k.op("vector", lambda E, sl=sl: E.bn_stats(b.st[:, 6:12], sl[:, 512:1024]))
        t_st = k.op("vector", lambda E: E.bn_aggr(b.mv[:, 0:2], b.st[:, :]),
                    inc="dve", drain=True)
        k.op("scalar", lambda E: E.activation(b.rstd[:, 0:1], b.mv[:, 1:2], AF.Ln,
                                              bias=b.eps[:, 0:1]),
             wait=[("dve", t_st)])
        t_rs = k.op("scalar", lambda E: E.activation(b.rstd[:, 1:2], b.rstd[:, 0:1],
                                                     AF.Exp, scale=-0.5),
                    inc="act", drain=True)
        t_ap2 = k.op("vector", lambda E, sl=sl: E.tensor_scalar(
            sl, sl, b.mv[:, 0:1], b.rstd[:, 1:2], ALU.subtract, ALU.mult),
            wait=[("act", t_rs)], inc="dve", drain=True)
        t_th = k.op("scalar",
                    lambda E, c2=c2, sl=sl: E.activation(
                        xtanh[:, c2 * 1024:(c2 + 1) * 1024], sl, AF.Tanh),
                    wait=[("dve", t_ap2)], inc="act", drain=True)
        t_trx = None
        for dc in range(NDC):
            t_trx = k.op("tensor",
                         lambda E, c2=c2, dc=dc: E.transpose(
                             pt[c2][:, dc * 128:(dc + 1) * 128],
                             xtanh[:, c2 * 1024 + dc * 128: c2 * 1024 + dc * 128 + 128],
                             ident_b[:, :]),
                         wait=([("act", t_th), ("dve", t_ct)] if dc == 0 else []),
                         inc=("pe" if dc == NDC - 1 else None))

        def evac_x(E, c2=c2):
            dst = x_fm[:, :].rearrange("q (a t) -> q a t", a=8)[:, :, c2 * 128:c2 * 128 + 128]
            return E.tensor_copy(dst, pt[c2][:, :].rearrange("q (a t) -> q a t", a=8))
        t_xfm = k.op("vector", evac_x, wait=[("pe", t_trx)], inc="dve")
    es_free = k.tok("pe")
    ytt_done = None
    for e in range(8):
        t_es = k.dma("gpsimd", wdst(esl, 8, 1024),
                     dr["ew"].ap()[e * 1024:(e + 1) * 1024, :]
                     .rearrange("(c p) m -> p c m", p=128),
                     wait=[("pe", es_free)], sem="es")
        t_pe_last = None
        evtok = [k.tok("dve")] * 2
        for hc in range(8):
            pp = hc % 2
            t_mm = None
            for kc in range(NDC):
                wl = []
                if kc == 0:
                    wl = [("dve", evtok[pp]), ("es", t_es)]
                    if e == 0 and hc == 0:
                        wl.append(("dve", t_xfm))
                t_mm = k.op("tensor",
                            lambda E, e=e, hc=hc, kc=kc, pp=pp: E.matmul(
                                ps[pp][:, 0:32],
                                esl[:, kc * 1024 + hc * 128: kc * 1024 + hc * 128 + 128],
                                x_fm[:, kc * 256 + 32 * e: kc * 256 + 32 * e + 32],
                                start=(kc == 0), stop=(kc == NDC - 1)),
                            wait=wl, inc=("pe" if kc == NDC - 1 else None))
            evtok[pp] = k.op("vector",
                             lambda E, e=e, hc=hc, pp=pp: E.tensor_scalar(
                                 ytt_fm[:, hc * 256 + e * 32: hc * 256 + e * 32 + 32],
                                 ps[pp][:, 0:32], ebfm[:, e * 8 + hc:e * 8 + hc + 1],
                                 None, ALU.add),
                             wait=[("pe", t_mm)], inc="dve")
            t_pe_last = t_mm
        es_free = t_pe_last
    t_ytt = k.tok("dve")
    for c2 in range(2):
        t_try = None
        for hc in range(8):
            t_try = k.op("tensor",
                         lambda E, c2=c2, hc=hc: E.transpose(
                             pt[c2][:, hc * 128:(hc + 1) * 128],
                             ytt_fm[:, hc * 256 + c2 * 128: hc * 256 + c2 * 128 + 128],
                             ident_b[:, :]),
                         wait=([("dve", t_ytt)] if hc == 0 else []),
                         inc=("pe" if hc == NDC - 1 else None))
        t_ys = k.op("vector",
                    lambda E, c2=c2: E.tensor_copy(ytst[:, :], pt[c2][:, :]),
                    wait=[("pe", t_try)], inc="dve")
        k.dma("sync", ytp_d.ap()[c2 * 128:(c2 + 1) * 128, :], ytst[:, :],
              wait=[("dve", t_ys)], sem="yt")
        t_ytt = k.op("vector", lambda E, c2=c2: E.engine_nop(),
                     wait=[("yt", k.tok("yt"))], inc="dve")
    t_yt = k.tok("yt")
    k.op("gpsimd", lambda E: E.collective_compute(
        "AllGather", ALU.bypass, replica_groups=[list(range(NCORES))],
        ins=[ytp_d.ap().opt()], outs=[ytf_d.ap().opt()]),
        wait=[("yt", t_yt)], inc="cc")
    t_ytl = k.dma("sync", yt_np[:, :].rearrange("q (c d) -> q c d", c=16),
                  ytf_d.ap().rearrange("(c q) d -> q c d", q=128),
                  wait=[("cc", 2)], sem="ytl")
    k.flush()

    radd_free = [0, 0]
    y_ps_free = [k.tok("dve")] * 2
    for tc in range(NTC):
        t_mm = None
        for npc in range(NTC):
            for n2 in range(2):
                wl = []
                if npc == 0:
                    wl = [("dve", y_ps_free[n2])]
                    if tc == 0:
                        wl += [("ytl", t_ytl), ("dve", t_ct)]
                t_mm = k.op("tensor",
                            lambda E, npc=npc, tc=tc, n2=n2: E.matmul(
                                ps[n2][:, :],
                                ct[:, npc * 2048 + tc * 128: npc * 2048 + tc * 128 + 128],
                                yt_np[:, npc * 1024 + n2 * 512: npc * 1024 + n2 * 512 + 512],
                                start=(npc == 0), stop=(npc == NTC - 1)),
                            wait=wl, inc=("pe" if npc == NTC - 1 and n2 == 1 else None))

        def sc_fn(E, dst, psum, half):
            return E.tensor_copy(dst, psum[:, :])
        t_ev = emit_epilogue_chunk(k, b, tc, [ps[0], ps[1]], sc_fn,
                                   dr["r_in"].ap(), dr["r_out"].ap(), t_mm, radd_free)
        y_ps_free[0] = t_ev
        y_ps_free[1] = t_ev
    k.op("sync", lambda E: E.nop(),
         wait=[("st0", k.tok("st0")), ("st1", k.tok("st1"))])
    k.flush()



def _mk_psums(nc):
    pw = [nc.alloc_psum_tensor(f"pw{i}", [128, 1024], F32) for i in range(3)]
    ptw = nc.alloc_psum_tensor("ptw", [128, 2048], BF16)
    ps = []
    for i in range(3):
        ps.append(pw[i][:, 0:512])
        ps.append(pw[i][:, 512:1024])
    pt = [ptw[:, 0:1024], ptw[:, 1024:2048]]
    return ps, pt, pw

A_WNAMES = ["wq", "wdn", "wku", "wvu", "wo", "ident", "cc0", "ss0", "perm0m"]
B_WNAMES = ["w1t", "w2t", "h1t", "h2t", "w3t", "s1", "s2", "hs1", "hs2", "s3fm", "ident"]


def _phase_a_nc(common):
    nc = bass.Bass("TRN2", target_bir_lowering=False)
    dr = {}
    dr["x"] = nc.dram_tensor("x", [T, DM], F32, kind="ExternalInput")
    for nm in A_WNAMES:
        dr[nm] = nc.dram_tensor(nm, list(common[nm].shape), F32, kind="ExternalInput")
    dr["r_out"] = nc.dram_tensor("r_out", [T, DM], F32, kind="ExternalOutput")
    k = K(nc)
    ps, pt, pw = _mk_psums(nc)
    emit_attn0(nc, k, dr, ps, pt, pw)
    return nc


def _phase_b_nc(common):
    nc = bass.Bass("TRN2", target_bir_lowering=False)
    dr = {}
    dr["r_in"] = nc.dram_tensor("r_in", [T, DM], F32, kind="ExternalInput")
    for nm in B_WNAMES:
        dr[nm] = nc.dram_tensor(nm, list(common[nm].shape), F32, kind="ExternalInput")
    dr["r_out"] = nc.dram_tensor("r_out", [T, DM], F32, kind="ExternalOutput")
    k = K(nc)
    ps, pt, pw = _mk_psums(nc)
    emit_ffn0(nc, k, dr, ps, pt)
    return nc


C_WNAMES = ["wdq", "wuq", "wqrp", "wdkv", "wuk", "wuv", "wkrp", "wot",
            "ident", "cc1", "ss1", "perm1m", "tril"]


def _phase_c_nc(common):
    nc = bass.Bass("TRN2", target_bir_lowering=False)
    dr = {}
    dr["r_in"] = nc.dram_tensor("r_in", [T, DM], F32, kind="ExternalInput")
    for nm in C_WNAMES:
        dr[nm] = nc.dram_tensor(nm, list(common[nm].shape), F32, kind="ExternalInput")
    dr["r_out"] = nc.dram_tensor("r_out", [T, DM], F32, kind="ExternalOutput")
    k = K(nc)
    ps, pt, pw = _mk_psums(nc)
    emit_attn1(nc, k, dr, ps, pt, pw)
    return nc


D_WNAMES = ["phit", "phib", "ident", "sel4", "ones1", "ebfm", "ew"]


def _phase_d_nc(common, shapes_percore):
    nc = bass.Bass("TRN2", target_bir_lowering=False)
    dr = {}
    dr["r_in"] = nc.dram_tensor("r_in", [T, DM], F32, kind="ExternalInput")
    for nm in ["phit", "phib", "ident", "sel4", "ones1"]:
        dr[nm] = nc.dram_tensor(nm, list(common[nm].shape), F32, kind="ExternalInput")
    dr["ebfm"] = nc.dram_tensor("ebfm", [128, 64], F32, kind="ExternalInput")
    dr["ew"] = nc.dram_tensor("ew", [8192, 1024], F32, kind="ExternalInput")
    dr["r_out"] = nc.dram_tensor("r_out", [T, DM], F32, kind="ExternalOutput")
    k = K(nc)
    ps, pt, pw = _mk_psums(nc)
    emit_ffn1(nc, k, dr, ps, pt)
    return nc


N_PHASES = 4


def _fused_nc(common):
    nc = bass.Bass("TRN2", target_bir_lowering=False)
    dr = {}
    dr["x"] = nc.dram_tensor("x", [T, DM], F32, kind="ExternalInput")
    for nm in set(A_WNAMES + B_WNAMES + C_WNAMES + ["phit", "phib", "sel4",
                                                    "ones1", "cc1", "ss1"]):
        dr[nm] = nc.dram_tensor(nm, list(common[nm].shape), F32, kind="ExternalInput")
    dr["ebfm"] = nc.dram_tensor("ebfm", [128, 64], F32, kind="ExternalInput")
    dr["ew"] = nc.dram_tensor("ew", [8192, 1024], F32, kind="ExternalInput")
    r1 = nc.dram_tensor("r1", [T, DM], F32)
    r2 = nc.dram_tensor("r2", [T, DM], F32)
    r3 = nc.dram_tensor("r3", [T, DM], F32)
    out = nc.dram_tensor("out", [T, DM], F32, kind="ExternalOutput")
    k = K(nc)
    ps, pt, pw = _mk_psums(nc)
    da = dict(dr); da["r_out"] = r1
    with nc.named_scope("phA"):
        emit_attn0(nc, k, da, ps, pt, pw)
    db = dict(dr); db["r_in"] = r1; db["r_out"] = r2
    with nc.named_scope("phB"):
        emit_ffn0(nc, k, db, ps, pt)
    dc = dict(dr); dc["r_in"] = r2; dc["r_out"] = r3
    with nc.named_scope("phC"):
        emit_attn1(nc, k, dc, ps, pt, pw)
    dd = dict(dr); dd["r_in"] = r3; dd["r_out"] = out
    with nc.named_scope("phD"):
        emit_ffn1(nc, k, dd, ps, pt)
    return nc


FUSED = True


def kernel(**inputs):
    common, per_core = host_prep(inputs)
    if FUSED:
        nc = _fused_nc(common)
        names = set(A_WNAMES + B_WNAMES + C_WNAMES + ["phit", "phib", "sel4",
                                                      "ones1", "cc1", "ss1"])
        in_maps = []
        for c in range(NCORES):
            m = {"x": per_core[c]["x"], "ebfm": per_core[c]["ebfm"],
                 "ew": per_core[c]["ew"]}
            for nm in names:
                m[nm] = common[nm]
            in_maps.append(m)
        res = run_bass_kernel_spmd(nc, in_maps, core_ids=list(range(NCORES)))
        return np.stack([res.results[c]["out"] for c in range(NCORES)], axis=0)
    return _kernel_multilaunch(common, per_core)


def _kernel_multilaunch(common, per_core):
    nc = _phase_a_nc(common)
    in_maps = []
    for c in range(NCORES):
        m = {"x": per_core[c]["x"]}
        for nm in A_WNAMES:
            m[nm] = common[nm]
        in_maps.append(m)
    res = run_bass_kernel_spmd(nc, in_maps, core_ids=list(range(NCORES)))
    r = [res.results[c]["r_out"] for c in range(NCORES)]
    if N_PHASES == 1:
        return np.stack(r, axis=0)
    nc = _phase_b_nc(common)
    in_maps = []
    for c in range(NCORES):
        m = {"r_in": r[c]}
        for nm in B_WNAMES:
            m[nm] = common[nm]
        in_maps.append(m)
    res = run_bass_kernel_spmd(nc, in_maps, core_ids=list(range(NCORES)))
    r = [res.results[c]["r_out"] for c in range(NCORES)]
    if N_PHASES == 2:
        return np.stack(r, axis=0)
    nc = _phase_c_nc(common)
    in_maps = []
    for c in range(NCORES):
        m = {"r_in": r[c]}
        for nm in C_WNAMES:
            m[nm] = common[nm]
        in_maps.append(m)
    res = run_bass_kernel_spmd(nc, in_maps, core_ids=list(range(NCORES)))
    r = [res.results[c]["r_out"] for c in range(NCORES)]
    if N_PHASES == 3:
        return np.stack(r, axis=0)
    nc = _phase_d_nc(common, None)
    in_maps = []
    for c in range(NCORES):
        m = {"r_in": r[c], "ebfm": per_core[c]["ebfm"], "ew": per_core[c]["ew"]}
        for nm in ["phit", "phib", "ident", "sel4", "ones1"]:
            m[nm] = common[nm]
        in_maps.append(m)
    res = run_bass_kernel_spmd(nc, in_maps, core_ids=list(range(NCORES)))
    r = [res.results[c]["r_out"] for c in range(NCORES)]
    return np.stack(r, axis=0)


# version 10
# speedup vs baseline: 1.0772x; 1.0772x over previous
import numpy as np

import concourse.bass as bass
import concourse.mybir as mybir
from concourse.bass_utils import run_bass_kernel_spmd

F32 = mybir.dt.float32
BF16 = mybir.dt.bfloat16
AF = mybir.ActivationFunctionType
ALU = mybir.AluOpType
AX = mybir.AxisListType

T = 2048
DM = 1024
NTC = 16
NDC = 8
NHEAD = 16
EPS = 1e-5
NCORES = 8

SBUF_BASE = 16512
SBUF_TOP = 229344


def _l0_col_perm():
    perm = np.zeros(1024, dtype=np.int64)
    for h in range(16):
        for i in range(32):
            perm[64 * h + i] = 64 * h + 2 * i
            perm[64 * h + 32 + i] = 64 * h + 2 * i + 1
    return perm


def _rope_tables_l0():
    inv = 1.0 / (10000.0 ** (np.arange(0, 64, 2, dtype=np.float32) / 64))
    ang = np.outer(np.arange(T, dtype=np.float32), inv)
    cos, sin = np.cos(ang), np.sin(ang)
    cc = np.zeros((128, T), dtype=np.float32)
    ss = np.zeros((128, T), dtype=np.float32)
    for r in range(128):
        i = r % 32
        cc[r] = cos[:, i]
        ss[r] = sin[:, i] if (r % 64) >= 32 else -sin[:, i]
    return cc, ss


def _rope_tables_l1():
    inv = 10000.0 ** (-np.arange(0, 32, 2, dtype=np.float32) / 32)
    ang = np.arange(T, dtype=np.float32)[:, None] * inv[None, :]
    cos, sin = np.cos(ang), np.sin(ang)
    cc = np.zeros((128, T), dtype=np.float32)
    ss = np.zeros((128, T), dtype=np.float32)
    for r in range(128):
        idx = r % 32
        i = idx % 16
        cc[r] = cos[:, i]
        ss[r] = sin[:, i] if idx >= 16 else -sin[:, i]
    return cc, ss


def _perm0():
    p = np.zeros((128, 128), dtype=np.float32)
    for i in range(128):
        j = (i // 64) * 64 + ((i + 32) % 64)
        p[j, i] = 1.0
    return p


def _perm1():
    p = np.zeros((128, 128), dtype=np.float32)
    for base in (0, 64):
        for i in range(16):
            p[base + 16 + i, base + i] = 1.0
            p[base + i, base + 16 + i] = 1.0
    return p


def _perm1c():
    p = np.zeros((128, 128), dtype=np.float32)
    for i in range(16):
        p[80 + i, 64 + i] = 1.0
        p[64 + i, 80 + i] = 1.0
    return p


def _rope_tables_l1c():
    inv = 10000.0 ** (-np.arange(0, 32, 2, dtype=np.float32) / 32)
    ang = np.arange(T, dtype=np.float32)[:, None] * inv[None, :]
    cos, sin = np.cos(ang), np.sin(ang)
    cc = np.ones((128, T), dtype=np.float32)
    ss = np.zeros((128, T), dtype=np.float32)
    for r in range(64, 96):
        idx = r - 64
        i = idx % 16
        cc[r] = cos[:, i]
        ss[r] = sin[:, i] if idx >= 16 else -sin[:, i]
    return cc, ss


def _sel4():
    s = np.zeros((32, 128), dtype=np.float32)
    for j in range(4):
        for p_ in range(32):
            s[p_, 32 * j + p_] = 1.0
    return s


def _scale_fm(s):
    c = s.shape[0] // 128
    return np.ascontiguousarray(s.reshape(c, 128).T)


def host_prep(inputs):
    cp = np.ascontiguousarray
    perm0 = _l0_col_perm()
    cc0, ss0 = _rope_tables_l0()
    cc1, ss1 = _rope_tables_l1()

    wqrT = inputs["L1_attn_W_QR"].T
    wqr_pad = np.zeros((512, 1024), dtype=np.float32)
    for h in range(16):
        base = (h // 2) * 128 + (h % 2) * 64
        for i in range(16):
            wqr_pad[:, base + i] = wqrT[:, 32 * h + 2 * i]
            wqr_pad[:, base + 16 + i] = wqrT[:, 32 * h + 2 * i + 1]

    wkrT = inputs["L1_attn_W_KR"].T
    wkr_pad = np.zeros((1024, 128), dtype=np.float32)
    for i in range(16):
        wkr_pad[:, i] = wkrT[:, 2 * i]
        wkr_pad[:, 16 + i] = wkrT[:, 2 * i + 1]

    eb = inputs["L1_ffn_expert_b"]

    common = {
        "ident": np.eye(128, dtype=np.float32),
        "cc0": cc0, "ss0": ss0, "cc1": cc1, "ss1": ss1,
        "perm0m": _perm0(), "perm1m": _perm1(), "perm1c": _perm1c(),
        "tril": np.triu(np.ones((128, 128), dtype=np.float32)),
        "ones1": np.ones((1, 128), dtype=np.float32),
        "sel4": _sel4(),
        "wq": cp(inputs["L0_attn_wq"][:, perm0]),
        "wdn": cp(inputs["L0_attn_w_down"]),
        "wku": cp(inputs["L0_attn_wk_up"][:, perm0]),
        "wvu": cp(inputs["L0_attn_wv_up"]),
        "wo": cp(inputs["L0_attn_wo"]),
        "w1t": cp(inputs["L0_ffn_w1_w"].T), "s1": _scale_fm(inputs["L0_ffn_w1_s"]),
        "w2t": cp(inputs["L0_ffn_w2_w"].T), "s2": _scale_fm(inputs["L0_ffn_w2_s"]),
        "h1t": cp(inputs["L0_ffn_h1_w"].T), "hs1": _scale_fm(inputs["L0_ffn_h1_s"]),
        "h2t": cp(inputs["L0_ffn_h2_w"].T), "hs2": _scale_fm(inputs["L0_ffn_h2_s"]),
        "w3t": cp(inputs["L0_ffn_w3_w"].T),
        "s3fm": _scale_fm(inputs["L0_ffn_w3_s"]),
        "wdq": cp(inputs["L1_attn_W_DQ"].T),
        "wuq": cp(inputs["L1_attn_W_UQ"].T),
        "wqrp": wqr_pad,
        "wdkv": cp(inputs["L1_attn_W_DKV"].T),
        "wuk": cp(inputs["L1_attn_W_UK"].T),
        "wuv": cp(inputs["L1_attn_W_UV"].T),
        "wkrp": wkr_pad,
        "wot": cp(inputs["L1_attn_W_O"].T),
        "cc1c": _rope_tables_l1c()[0], "ss1c": _rope_tables_l1c()[1],
        "phit": cp(inputs["L1_ffn_phi_w"].T),
        "phib": cp(inputs["L1_ffn_phi_b"][None, :]),
    }
    per_core = []
    for c in range(NCORES):
        ew = cp(inputs["L1_ffn_expert_w"][8 * c: 8 * c + 8]).reshape(8192, 1024)
        ebs = eb[8 * c: 8 * c + 8]
        eb_fm = np.concatenate(
            [np.ascontiguousarray(ebs[e].reshape(8, 128).T) for e in range(8)], axis=1)
        per_core.append({
            "x": cp(inputs["L0_attn_x"][c]),
            "ew": ew,
            "ebfm": cp(eb_fm),
        })
    return common, per_core


ENGS = ("sync", "vector", "scalar", "tensor", "gpsimd")


class K:
    def __init__(self, nc):
        self.nc = nc
        self.q = {e: [] for e in ENGS}
        self.sems = {}
        self.cnt = {}

    def sem(self, n):
        if n not in self.sems:
            self.sems[n] = self.nc.alloc_semaphore(n)
            self.cnt[n] = 0
        return self.sems[n]

    def tok(self, n):
        self.sem(n)
        return self.cnt[n]

    def op(self, eng, build, wait=(), inc=None, inc_by=1, drain=False):
        tok = None
        if inc is not None:
            self.sem(inc)
            self.cnt[inc] += inc_by
            tok = self.cnt[inc]
        sem = self.sems[inc] if inc is not None else None
        for (s, t) in wait:
            self.sem(s)

        def fn(E, build=build, wait=tuple(wait), sem=sem, inc_by=inc_by, drain=drain):
            if drain:
                E.drain()
            for (s, t) in wait:
                if t > 0:
                    E.wait_ge(self.sems[s], t)
            ins = build(E)
            if sem is not None:
                ins.then_inc(sem, inc_by)

        self.q[eng].append(fn)
        return tok

    def dma(self, eng, out, in_, wait=(), sem=None, drain=False):
        sem = sem or ("sg" if eng == "gpsimd" else ("sa" if eng == "scalar" else "sd"))
        return self.op(eng, lambda E: E.dma_start(out=out, in_=in_),
                       wait=wait, inc=sem, inc_by=16, drain=drain)

    def flush(self):
        with self.nc.Block() as blk:
            for name in ENGS:
                if not self.q[name]:
                    continue
                fns = self.q[name]
                self.q[name] = []

                def body(E, fns=fns):
                    for f in fns:
                        f(E)
                getattr(blk, name)(body)


class Arena:

    def __init__(self, nc, base=SBUF_BASE):
        self.nc = nc
        self.off = base
        self.offs = {}

    @staticmethod
    def bpp(shape, dtype):
        n = 1
        for s in shape[1:]:
            n *= s
        return n * (2 if dtype in (BF16, mybir.dt.float16) else 4)

    def alloc(self, name, shape, dtype, align=32):
        off = (self.off + align - 1) // align * align
        h = self.nc.alloc_sbuf_tensor_at(name, shape, dtype, offset=off)
        self.offs[name] = off
        self.off = off + self.bpp(shape, dtype)
        assert self.off <= SBUF_TOP, f"SBUF overflow at {name}: {self.off}"
        return h

    def at(self, name, shape, dtype, offset):
        assert offset + self.bpp(shape, dtype) <= SBUF_TOP, name
        self.offs[name] = offset
        return self.nc.alloc_sbuf_tensor_at(name, shape, dtype, offset=offset)


class LNBufs:

    def __init__(self, a, tag):
        self.rb = [a.alloc(f"{tag}_rb{i}", [128, 1024], F32) for i in range(2)]
        self.st = a.alloc(f"{tag}_st", [128, 12], F32)
        self.mv = a.alloc(f"{tag}_mv", [128, 2], F32)
        self.rstd = a.alloc(f"{tag}_rstd", [128, 2], F32)
        self.ytm = [a.alloc(f"{tag}_ytm{i}", [128, 1024], BF16) for i in range(2)]
        self.ob = a.at(f"{tag}_ob", [128, 1024], F32, a.offs[f"{tag}_ytm0"])
        self.eps = a.alloc(f"{tag}_eps", [128, 1], F32)


def fm_view(h, dc_count, tcols, dc, csl):
    return h[:, dc * tcols + csl.start: dc * tcols + csl.stop]


def emit_ln_stream(k, b, psT, r_dram, y_fm, y_tm, ident_b, row0=0, nchunks=NTC,
                   first_wait=(), rb_guard=None):
    fw = list(first_wait)
    evac_tok = [0, 0]
    rb_guard = list(rb_guard) if rb_guard else [None, None]
    tr_evac = [0, 0]
    use_tok = [0, 0]
    ntok = 128 * nchunks
    for tc in range(nchunks):
        p = tc % 2
        r0 = row0 + tc * 128
        ld_wait = [("dve", evac_tok[p])]
        if rb_guard[p] is not None:
            ld_wait.append(rb_guard[p])
            rb_guard[p] = None
        t_load = k.dma("sync", b.rb[p][:, :], r_dram[r0:r0 + 128, :],
                       wait=ld_wait, sem=f"rb{p}")
        k.op("vector", lambda E, p=p: E.bn_stats(b.st[:, 0:6], b.rb[p][:, 0:512]),
             wait=[(f"rb{p}", t_load)], drain=True)
        k.op("vector", lambda E, p=p: E.bn_stats(b.st[:, 6:12], b.rb[p][:, 512:1024]))
        t_st = k.op("vector", lambda E: E.bn_aggr(b.mv[:, 0:2], b.st[:, :]),
                    inc="dve", drain=True)
        k.op("scalar", lambda E: E.activation(b.rstd[:, 0:1], b.mv[:, 1:2], AF.Ln,
                                              bias=b.eps[:, 0:1]),
             wait=[("dve", t_st)])
        t_rs = k.op("scalar", lambda E: E.activation(b.rstd[:, 1:2], b.rstd[:, 0:1],
                                                     AF.Exp, scale=-0.5),
                    inc="act", drain=True)
        dst = y_tm[:, tc * 1024:(tc + 1) * 1024] if y_tm is not None else b.ytm[p][:, :]
        t_app = k.op("vector",
                     lambda E, p=p, dst=dst: E.tensor_scalar(
                         dst, b.rb[p][:, :], b.mv[:, 0:1], b.rstd[:, 1:2],
                         ALU.subtract, ALU.mult),
                     wait=[("act", t_rs), ("pe", use_tok[p])], inc="dve")
        evac_tok[p] = t_app
        ps = psT[p]
        t_tr = None
        for dc in range(NDC):
            wl = []
            if dc == 0:
                wl = [("dve", t_app), ("dve", tr_evac[p])] + fw
                fw = []
            t_tr = k.op("tensor",
                        lambda E, dc=dc, ps=ps, src=dst: E.transpose(
                            ps[:, dc * 128:(dc + 1) * 128], src[:, dc * 128:(dc + 1) * 128],
                            ident_b[:, :]),
                        wait=wl, inc=("pe" if dc == NDC - 1 else None))
        use_tok[p] = t_tr

        def evac(E, ps=ps, tc=tc):
            dst_ap = y_fm[:, :].rearrange("q (a t) -> q a t", a=8)[:, :, tc * 128:tc * 128 + 128]
            return E.tensor_copy(dst_ap, ps[:, :].rearrange("q (a t) -> q a t", a=8))
        tr_evac[p] = k.op("vector", evac, wait=[("pe", t_tr)], inc="dve")
    return k.tok("dve")


def wsrc(dr_t):
    return dr_t.ap().rearrange("(c p) m -> p c m", p=128)


def wdst(wbuf, kchunks, mtot):
    return wbuf[:, 0:kchunks * mtot].rearrange("p (c m) -> p c m", c=kchunks)


def emit_proj(k, wbuf_sb, kchunks, mchunks, rhs_fm, rhs_tcols, out_sb, out_tcols,
              psums, n_ttiles, ttile, extra_wait=()):
    toks = [k.tok("dve")] * len(psums)
    first = True
    for mc in range(mchunks):
        for tt in range(n_ttiles):
            i = (mc * n_ttiles + tt) % len(psums)
            ps = psums[i]
            t_mm = None
            for kc in range(kchunks):
                w_sl = wbuf_sb[:, (kc * mchunks + mc) * 128:(kc * mchunks + mc) * 128 + 128]
                r_sl = rhs_fm[:, kc * rhs_tcols + tt * ttile: kc * rhs_tcols + tt * ttile + ttile]
                wl = []
                if kc == 0:
                    wl = [("dve", toks[i])]
                    if first:
                        wl += list(extra_wait)
                t_mm = k.op("tensor",
                            lambda E, ps=ps, w_sl=w_sl, r_sl=r_sl, kc=kc, ttile=ttile:
                            E.matmul(ps[:, 0:ttile], w_sl, r_sl,
                                     start=(kc == 0), stop=(kc == kchunks - 1)),
                            wait=wl, inc=("pe" if kc == kchunks - 1 else None))
            first = False
            dst = out_sb[:, mc * out_tcols + tt * ttile: mc * out_tcols + tt * ttile + ttile]
            toks[i] = k.op("vector",
                           lambda E, dst=dst, ps=ps, ttile=ttile:
                           E.tensor_copy(dst, ps[:, 0:ttile]),
                           wait=[("pe", t_mm)], inc="dve")
    return k.tok("dve")


def emit_rope(k, tgt, nchunks, cc_b, ss_b, perm_b, pst, scr, first_wait=(),
              ps_free=0):
    first = True
    for ch in range(nchunks):
        for j4 in range(4):
            sl = slice(ch * T + j4 * 512, ch * T + j4 * 512 + 512)
            tsl = slice(j4 * 512, j4 * 512 + 512)
            wl = [("dve", ps_free)]
            if first:
                wl += list(first_wait)
                first = False
            t_sw = k.op("tensor",
                        lambda E, sl=sl: E.matmul(pst[:, :], perm_b[:, :], tgt[:, sl],
                                                  start=True, stop=True),
                        wait=wl, inc="pe")
            k.op("vector", lambda E, tsl=tsl: E.tensor_tensor(
                scr[:, 0:512], pst[:, :], ss_b[:, tsl], ALU.mult),
                wait=[("pe", t_sw)])
            k.op("vector", lambda E, sl=sl, tsl=tsl: E.tensor_tensor(
                scr[:, 512:1024], tgt[:, sl], cc_b[:, tsl], ALU.mult))
            ps_free = k.op("vector", lambda E, sl=sl: E.tensor_tensor(
                tgt[:, sl], scr[:, 0:512], scr[:, 512:1024], ALU.add),
                inc="dve", drain=True)
    return ps_free


def emit_epilogue_chunk(k, b, tc, psum_pair, scale_fn, r_dram, out_dram,
                        wait_pe, radd_free):
    p = tc % 2
    t_ld = k.dma("sync", b.rb[p][:, :], r_dram[tc * 128:(tc + 1) * 128, :],
                 wait=[(f"st{p}", radd_free[p])], sem=f"rb{p}")
    for half in range(2):
        k.op("vector",
             lambda E, half=half: scale_fn(E, b.ob[:, half * 512:half * 512 + 512],
                                           psum_pair[half], half),
             wait=([("pe", wait_pe)] if half == 0 else []), drain=(half == 0))
    t_ev = k.op("vector", lambda E: E.bn_stats(b.st[:, 0:6], b.ob[:, 0:512]),
                inc="dve", drain=True)
    k.op("vector", lambda E: E.bn_stats(b.st[:, 6:12], b.ob[:, 512:1024]))
    t_st = k.op("vector", lambda E: E.bn_aggr(b.mv[:, 0:2], b.st[:, :]),
                inc="dve", drain=True)
    k.op("scalar", lambda E: E.activation(b.rstd[:, 0:1], b.mv[:, 1:2], AF.Ln,
                                          bias=b.eps[:, 0:1]),
         wait=[("dve", t_st)])
    t_rs = k.op("scalar", lambda E: E.activation(b.rstd[:, 1:2], b.rstd[:, 0:1],
                                                 AF.Exp, scale=-0.5),
                inc="act", drain=True)
    k.op("vector", lambda E: E.tensor_scalar(b.ob[:, :], b.ob[:, :], b.mv[:, 0:1],
                                             b.rstd[:, 1:2], ALU.subtract, ALU.mult),
         wait=[("act", t_rs), (f"rb{p}", t_ld)])
    t_add = k.op("vector", lambda E, p=p: E.tensor_tensor(
        b.rb[p][:, :], b.ob[:, :], b.rb[p][:, :], ALU.add), inc="dve", drain=True)
    t_out = k.dma("sync", out_dram[tc * 128:(tc + 1) * 128, :], b.rb[p][:, :],
                  wait=[("dve", t_add)], sem=f"st{p}")
    radd_free[p] = t_out
    return t_ev


def emit_attn0(nc, k, dr, ps, pt, pw):
    a = Arena(nc)
    ident_b = a.alloc("A_ident", [128, 128], BF16)
    cc0_b = a.alloc("A_cc0", [128, T], BF16)
    ss0_b = a.alloc("A_ss0", [128, T], BF16)
    perm_b = a.alloc("A_perm", [128, 128], BF16)
    v_tm = a.alloc("A_v", [128, NTC * 1040], BF16)
    q_sb = a.alloc("A_q", [128, 8 * T], BF16)
    k_sb = a.alloc("A_k", [128, 8 * T], BF16)
    attnout = a.alloc("A_ao", [128, NTC * 1024], BF16)
    lat = a.at("A_lat", [128, 4 * T], BF16, a.offs["A_ao"])
    ov1 = a.off
    y0_fm = a.at("A_y0fm", [128, 8 * T], BF16, ov1)
    ebuf = [a.at(f"A_e{i}", [128, 16 * 512], BF16, ov1 + i * 16 * 512 * 2) for i in range(2)]
    ao_fm = a.at("A_aofm", [128, 8 * T], BF16, ov1)
    a.off = ov1 + 8 * T * 2
    wbuf = a.alloc("A_wbuf", [128, 8192], BF16)
    scr = a.alloc("A_scr", [128, 1024], BF16)
    rcp = a.alloc("A_rcp", [128, 2], F32)
    b = LNBufs(a, "A")


    for nm, dst in [("ident", ident_b), ("cc0", cc0_b), ("ss0", ss0_b), ("perm0m", perm_b)]:
        k.dma("gpsimd", dst[:, :], dr[nm].ap(), sem="scst")
    t_consts = k.tok("scst")
    t_wq = k.dma("gpsimd", wdst(wbuf, 8, 1024), wsrc(dr["wq"]))
    t_ones = k.op("vector", lambda E: E.memset(
        v_tm[:, :].rearrange("q (h c) -> q h c", c=65)[:, :, 64:65], 1.0), inc="dve")
    k.op("vector", lambda E: E.memset(b.eps[:, :], EPS))
    emit_ln_stream(k, b, pt, dr["x"].ap(), y0_fm, None, ident_b,
                   first_wait=[("scst", t_consts)])
    k.flush()

    emit_proj(k, wbuf, 8, 8, y0_fm, T, q_sb, T, ps[0:2], 4, 512,
              extra_wait=[("sg", t_wq)])
    t_qdone = k.tok("dve")
    t_wdn = k.dma("gpsimd", wdst(wbuf, 8, 512), wsrc(dr["wdn"]),
                  wait=[("dve", t_qdone)])
    emit_proj(k, wbuf, 8, 4, y0_fm, T, lat, T, ps[2:4], 4, 512,
              extra_wait=[("sg", t_wdn)])
    t_latdone = k.tok("dve")
    t_wku = k.dma("gpsimd", wdst(wbuf, 4, 1024), wsrc(dr["wku"]),
                  wait=[("dve", t_latdone)])
    emit_proj(k, wbuf, 4, 8, lat, T, k_sb, T, ps[0:2], 4, 512,
              extra_wait=[("sg", t_wku), ("dve", t_latdone)])
    t_kdone = k.tok("dve")
    t_wvu = k.dma("gpsimd", wdst(wbuf, 4, 1024), wsrc(dr["wvu"]),
                  wait=[("dve", t_kdone)])
    vp_free = [k.tok("dve")] * 2
    for tc in range(NTC):
        for n2 in range(2):
            i = n2
            t_mm = None
            for kc in range(4):
                wl = []
                if kc == 0:
                    wl = [("dve", vp_free[i])]
                    if tc == 0 and n2 == 0:
                        wl += [("sg", t_wvu), ("dve", t_ones)]
                t_mm = k.op("tensor",
                            lambda E, i=i, kc=kc, tc=tc, n2=n2: E.matmul(
                                ps[4 + i][:, :], lat[:, kc * T + tc * 128: kc * T + tc * 128 + 128],
                                wbuf[:, kc * 1024 + n2 * 512: kc * 1024 + n2 * 512 + 512],
                                start=(kc == 0), stop=(kc == 3)),
                            wait=wl, inc=("pe" if kc == 3 else None))

            def evac_v(E, tc=tc, n2=n2, i=i):
                lo = tc * 1040 + n2 * 8 * 65
                dst = v_tm[:, lo:lo + 8 * 65].rearrange("q (h c) -> q h c", c=65)[:, :, 0:64]
                return E.tensor_copy(dst, ps[4 + i][:, :].rearrange("q (a c) -> q a c", a=8))
            vp_free[i] = k.op("vector", evac_v, wait=[("pe", t_mm)], inc="dve")
    t_vdone = k.tok("dve")
    t_rope = emit_rope(k, q_sb, 8, cc0_b, ss0_b, perm_b, ps[5], scr,
                       first_wait=[("dve", t_vdone)], ps_free=t_vdone)
    emit_rope(k, k_sb, 8, cc0_b, ss0_b, perm_b, ps[5], scr, ps_free=t_rope)
    k.flush()

    qk_free = [k.tok("act")] * 2
    e_wfree = [k.tok("pe")] * 2
    pv_free = [k.tok("dve")] * 2
    tiles = [(h, j) for h in range(NHEAD) for j in range(4)]
    exp_tok = {}

    def emit_qk_a(idx):
        h, j = tiles[idx]
        ch, base = h // 2, 64 * (h % 2)
        par = idx % 2
        nkt = min(4 * j + 4, 16)
        for kt0 in range(0, nkt, 2):
            pb2 = (kt0 // 2) % 2
            t_qk = None
            for kt in (kt0, kt0 + 1):
                t_qk = k.op("tensor",
                            lambda E, pb2=pb2, kt=kt, kt0=kt0, j=j, ch=ch, base=base: E.matmul(
                                pw[pb2][:, (kt - kt0) * 512:(kt - kt0) * 512 + 512],
                                k_sb[base:base + 64, ch * T + kt * 128: ch * T + kt * 128 + 128],
                                q_sb[base:base + 64, ch * T + j * 512: ch * T + j * 512 + 512],
                                start=True, stop=True),
                            wait=([("act", qk_free[pb2])] if kt == kt0 else []),
                            inc="pe")
            qk_free[pb2] = k.op("scalar",
                                lambda E, pb2=pb2, kt0=kt0, par=par: E.activation(
                                    ebuf[par][:, kt0 * 512: kt0 * 512 + 1024],
                                    pw[pb2][:, :], AF.Exp, scale=0.125),
                                wait=[("pe", t_qk),
                                      ("pe", e_wfree[par] if kt0 == 0 else 0)],
                                inc="act")
        exp_tok[idx] = qk_free[((nkt - 2) // 2) % 2]

    def emit_pv_a(idx):
        h, j = tiles[idx]
        par = idx % 2
        t_exp = exp_tok[idx]
        for qc in range(4):
            qcg = 4 * j + qc
            nk2 = min(2 * (qcg // 2) + 2, 16)
            pvp = qc % 2
            t_pv = None
            for kt in range(nk2):
                t_pv = k.op("tensor",
                            lambda E, pvp=pvp, kt=kt, qc=qc, h=h, par=par, nk2=nk2: E.matmul(
                                ps[4 + pvp][:, 0:65],
                                ebuf[par][:, kt * 512 + qc * 128: kt * 512 + qc * 128 + 128],
                                v_tm[:, kt * 1040 + 65 * h: kt * 1040 + 65 * h + 65],
                                start=(kt == 0), stop=(kt == nk2 - 1)),
                            wait=([("act", t_exp), ("dve", pv_free[pvp])] if kt == 0 else []),
                            inc=("pe" if kt == nk2 - 1 else None))
            if qc == 3:
                e_wfree[par] = t_pv
            t_rc = k.op("vector", lambda E, pvp=pvp: E.reciprocal(
                rcp[:, pvp:pvp + 1], ps[4 + pvp][:, 64:65]),
                wait=[("pe", t_pv)], inc="dve")
            pv_free[pvp] = k.op("vector",
                                lambda E, pvp=pvp, qcg=qcg, h=h: E.tensor_scalar_mul(
                                    attnout[:, qcg * 1024 + 64 * h: qcg * 1024 + 64 * h + 64],
                                    ps[4 + pvp][:, 0:64], rcp[:, pvp:pvp + 1]),
                                wait=[("dve", t_rc)], inc="dve")

    emit_qk_a(0)
    for i in range(len(tiles)):
        if i + 1 < len(tiles):
            emit_qk_a(i + 1)
        emit_pv_a(i)
    k.flush()

    t_wo = k.dma("gpsimd", wdst(wbuf, 8, 1024), wsrc(dr["wo"]))
    tr_free = [k.tok("dve")] * 2
    for tc in range(NTC):
        p = tc % 2
        t_tr = None
        for dc in range(NDC):
            t_tr = k.op("tensor",
                        lambda E, p=p, dc=dc, tc=tc: E.transpose(
                            pt[p][:, dc * 128:(dc + 1) * 128],
                            attnout[:, tc * 1024 + dc * 128: tc * 1024 + dc * 128 + 128],
                            ident_b[:, :]),
                        wait=([("dve", tr_free[p])] if dc == 0 else []),
                        inc=("pe" if dc == NDC - 1 else None))

        def evac_t(E, p=p, tc=tc):
            dst = ao_fm[:, :].rearrange("q (a t) -> q a t", a=8)[:, :, tc * 128:tc * 128 + 128]
            return E.tensor_copy(dst, pt[p][:, :].rearrange("q (a t) -> q a t", a=8))
        tr_free[p] = k.op("vector", evac_t, wait=[("pe", t_tr)], inc="dve")
    t_trd = k.tok("dve")
    radd_free = [0, 0]
    wo_ps_free = [k.tok("dve")] * 2
    for tc in range(NTC):
        t_mm = None
        for kc in range(NDC):
            for n2 in range(2):
                wl = []
                if kc == 0:
                    wl = [("dve", wo_ps_free[n2])]
                    if tc == 0:
                        wl += [("sg", t_wo), ("dve", t_trd)]
                t_mm = k.op("tensor",
                            lambda E, kc=kc, tc=tc, n2=n2: E.matmul(
                                ps[n2][:, :],
                                ao_fm[:, kc * T + tc * 128: kc * T + tc * 128 + 128],
                                wbuf[:, kc * 1024 + n2 * 512: kc * 1024 + n2 * 512 + 512],
                                start=(kc == 0), stop=(kc == NDC - 1)),
                            wait=wl, inc=("pe" if kc == NDC - 1 and n2 == 1 else None))

        def sc_fn(E, dst, psum, half):
            return E.tensor_copy(dst, psum[:, :])
        t_ev = emit_epilogue_chunk(k, b, tc, [ps[0], ps[1]], sc_fn,
                                   dr["x"].ap(), dr["r_out"].ap(), t_mm, radd_free)
        wo_ps_free[0] = t_ev
        wo_ps_free[1] = t_ev
    k.op("sync", lambda E: E.nop(),
         wait=[("st0", k.tok("st0")), ("st1", k.tok("st1"))])
    k.flush()



def emit_ffn0(nc, k, dr, ps, pt):
    a = Arena(nc)
    ident_b = a.alloc("B_ident", [128, 128], BF16)
    ident_f = a.alloc("B_identf", [128, 128], F32)
    s1c = a.alloc("B_s1", [128, 32], F32)
    s2c = a.alloc("B_s2", [128, 32], F32)
    hs1c = a.alloc("B_hs1", [128, 32], F32)
    hs2c = a.alloc("B_hs2", [128, 32], F32)
    s3c = a.alloc("B_s3", [128, 8], F32)
    silu_buf = a.alloc("B_silu", [128, 1024], F32)
    b = LNBufs(a, "B")
    y1 = a.alloc("B_y1", [128, 8 * 1024], BF16)
    h1h = a.alloc("B_h1h", [128, 32 * 1024], BF16)
    g1h = a.alloc("B_g1h", [128, 32 * 1024], BF16)
    wsl = [a.alloc(f"B_wsl{i}", [128, 8 * 512], BF16) for i in range(4)]
    hsl = [a.at(f"B_hsl{i}", [128, 32 * 256], BF16, a.offs["B_wsl0"] + i * 32 * 256 * 2)
           for i in range(2)]
    w3sl = [a.at(f"B_w3sl{i}", [128, 32 * 128], BF16,
                 a.offs["B_wsl0"] + i * 32 * 128 * 2) for i in range(2)]
    out_fm = a.at("B_outfm", [128, 8 * 1024], F32, a.offs["B_g1h"])


    k.dma("gpsimd", ident_b[:, :], dr["ident"].ap(), sem="scst")
    k.dma("gpsimd", ident_f[:, :], dr["ident"].ap(), sem="scst")
    k.dma("gpsimd", s1c[:, :], dr["s1"].ap(), sem="scst")
    k.dma("gpsimd", s2c[:, :], dr["s2"].ap(), sem="scst")
    k.dma("gpsimd", hs1c[:, :], dr["hs1"].ap(), sem="scst")
    k.dma("gpsimd", hs2c[:, :], dr["hs2"].ap(), sem="scst")
    k.dma("gpsimd", s3c[:, :], dr["s3fm"].ap(), sem="scst")
    t_cst = k.tok("scst")
    k.op("vector", lambda E: E.memset(b.eps[:, :], EPS))

    radd_free = [0, 0]
    for half in range(2):
        emit_ln_stream(k, b, pt, dr["r_in"].ap(), y1, None, ident_b,
                       row0=half * 1024, nchunks=8,
                       first_wait=[("scst", t_cst)],
                       rb_guard=[("st0", radd_free[0]), ("st1", radd_free[1])])
        k.flush()

        slab_free = {0: k.tok("pe"), 1: k.tok("pe")}
        t_w = {}
        for og in range(9):
            par = og % 2
            if og < 8:
                t_w[(og, 1)] = k.dma(
                    "gpsimd", wdst(wsl[par], 8, 512),
                    dr["w1t"].ap()[:, og * 512:(og + 1) * 512]
                    .rearrange("(c p) m -> p c m", p=128),
                    wait=[("pe", slab_free[par])], sem=f"wA{par}")
                t_w[(og, 2)] = k.dma(
                    "gpsimd", wdst(wsl[2 + par], 8, 512),
                    dr["w2t"].ap()[:, og * 512:(og + 1) * 512]
                    .rearrange("(c p) m -> p c m", p=128),
                    sem=f"wB{par}")
            if og == 0:
                continue
            ogc = og - 1
            parc = ogc % 2
            psfree = [k.tok("act"), k.tok("act"), k.tok("dve"), k.tok("dve")]
            silu_free = [k.tok("dve")] * 2
            for oc in range(4):
                o = ogc * 4 + oc
                t_a = [None, None]
                t_b = [None, None]
                for tt in range(2):
                    for kc in range(8):
                        wl = []
                        if kc == 0:
                            wl = [("act", psfree[tt]), (f"wA{parc}", t_w[(ogc, 1)])]
                        t_a[tt] = k.op("tensor",
                                       lambda E, tt=tt, kc=kc, oc=oc, parc=parc:
                                       E.matmul(ps[tt][:, :],
                                                wsl[parc][:, (kc * 4 + oc) * 128:(kc * 4 + oc) * 128 + 128],
                                                y1[:, kc * 1024 + tt * 512: kc * 1024 + tt * 512 + 512],
                                                start=(kc == 0), stop=(kc == 7)),
                                       wait=wl, inc=("pe" if kc == 7 else None))
                for tt in range(2):
                    for kc in range(8):
                        wl = []
                        if kc == 0:
                            wl = [("dve", psfree[2 + tt]), (f"wB{parc}", t_w[(ogc, 2)])]
                        t_b[tt] = k.op("tensor",
                                       lambda E, tt=tt, kc=kc, oc=oc, parc=parc:
                                       E.matmul(ps[2 + tt][:, :],
                                                wsl[2 + parc][:, (kc * 4 + oc) * 128:(kc * 4 + oc) * 128 + 128],
                                                y1[:, kc * 1024 + tt * 512: kc * 1024 + tt * 512 + 512],
                                                start=(kc == 0), stop=(kc == 7)),
                                       wait=wl, inc=("pe" if kc == 7 else None))
                for tt in range(2):
                    psfree[tt] = k.op("scalar",
                                      lambda E, tt=tt, o=o: E.activation(
                                          silu_buf[:, tt * 512:(tt + 1) * 512],
                                          ps[tt][:, :], AF.Silu, scale=s1c[:, o:o + 1]),
                                      wait=[("pe", t_a[tt]), ("dve", silu_free[tt])],
                                      inc="act", drain=True)
                    silu_free[tt] = k.op("vector",
                                         lambda E, tt=tt, o=o: E.scalar_tensor_tensor(
                                             h1h[:, o * 1024 + tt * 512: o * 1024 + tt * 512 + 512],
                                             ps[2 + tt][:, :], s2c[:, o:o + 1],
                                             silu_buf[:, tt * 512:(tt + 1) * 512],
                                             ALU.mult, ALU.mult),
                                         wait=[("pe", t_b[tt]), ("act", psfree[tt])],
                                         inc="dve")
                    psfree[2 + tt] = silu_free[tt]
            slab_free[parc] = k.tok("pe")
        k.flush()

        for stage in range(2):
            src = h1h if stage == 0 else g1h
            dsts = g1h if stage == 0 else h1h
            wname = "h1t" if stage == 0 else "h2t"
            hsc = hs1c if stage == 0 else hs2c
            slab_free = {0: k.tok("pe"), 1: k.tok("pe")}
            t_w = {}
            for og in range(17):
                par = og % 2
                if og < 16:
                    t_w[og] = k.dma(
                        "gpsimd", wdst(hsl[par], 32, 256),
                        dr[wname].ap()[:, og * 256:(og + 1) * 256]
                        .rearrange("(c p) m -> p c m", p=128),
                        wait=[("pe", slab_free[par])], sem=f"wh{par}")
                if og == 0:
                    continue
                ogc = og - 1
                parc = ogc % 2
                psfree = [k.tok("act")] * 4
                for oc2 in range(2):
                    o = ogc * 2 + oc2
                    for tt in range(2):
                        pi = oc2 * 2 + tt
                        t_mm = None
                        for kc in range(32):
                            wl = []
                            if kc == 0:
                                wl = [("act", psfree[pi]), (f"wh{parc}", t_w[ogc])]
                            t_mm = k.op("tensor",
                                        lambda E, pi=pi, kc=kc, oc2=oc2, tt=tt, parc=parc, src=src:
                                        E.matmul(ps[pi][:, :],
                                                 hsl[parc][:, (kc * 2 + oc2) * 128:(kc * 2 + oc2) * 128 + 128],
                                                 src[:, kc * 1024 + tt * 512: kc * 1024 + tt * 512 + 512],
                                                 start=(kc == 0), stop=(kc == 31)),
                                        wait=wl, inc=("pe" if kc == 31 else None))
                        psfree[pi] = k.op("scalar",
                                          lambda E, pi=pi, o=o, tt=tt, dsts=dsts, hsc=hsc:
                                          E.activation(
                                              dsts[:, o * 1024 + tt * 512: o * 1024 + tt * 512 + 512],
                                              ps[pi][:, :], AF.Gelu, scale=hsc[:, o:o + 1]),
                                          wait=[("pe", t_mm)], inc="act")
                slab_free[parc] = k.tok("pe")
            k.flush()

        g2h = h1h
        slab_free = {0: k.tok("pe"), 1: k.tok("pe")}
        t_w = {}
        for oc in range(9):
            par = oc % 2
            if oc < 8:
                t_w[oc] = k.dma(
                    "gpsimd", wdst(w3sl[par], 32, 128),
                    dr["w3t"].ap()[:, oc * 128:(oc + 1) * 128]
                    .rearrange("(c p) m -> p c m", p=128),
                    wait=[("pe", slab_free[par])], sem=f"w3{par}")
            if oc == 0:
                continue
            occ = oc - 1
            parc = occ % 2
            psfree = [k.tok("dve")] * 2
            for tt in range(2):
                t_mm = None
                for kc in range(32):
                    wl = []
                    if kc == 0:
                        wl = [("dve", psfree[tt]), (f"w3{parc}", t_w[occ])]
                    t_mm = k.op("tensor",
                                lambda E, tt=tt, kc=kc, parc=parc:
                                E.matmul(ps[tt][:, :],
                                         w3sl[parc][:, kc * 128:kc * 128 + 128],
                                         g2h[:, kc * 1024 + tt * 512: kc * 1024 + tt * 512 + 512],
                                         start=(kc == 0), stop=(kc == 31)),
                                wait=wl, inc=("pe" if kc == 31 else None))
                psfree[tt] = k.op("vector",
                                  lambda E, tt=tt, occ=occ: E.tensor_scalar_mul(
                                      out_fm[:, occ * 1024 + tt * 512: occ * 1024 + tt * 512 + 512],
                                      ps[tt][:, :], s3c[:, occ:occ + 1]),
                                  wait=[("pe", t_mm)], inc="dve")
            slab_free[parc] = k.tok("pe")
        t_w3d = k.tok("dve")
        k.flush()

        ps_free_t = k.tok("dve")
        for tcl in range(8):
            tc = half * 8 + tcl
            t_tr = None
            for dc in range(NDC):
                wl = []
                if dc == 0:
                    wl = [("dve", ps_free_t)]
                    if tcl == 0:
                        wl.append(("dve", t_w3d))
                t_tr = k.op("tensor",
                            lambda E, dc=dc, tcl=tcl: E.transpose(
                                ps[4 + dc // 4][:, (dc % 4) * 128:(dc % 4) * 128 + 128],
                                out_fm[:, dc * 1024 + tcl * 128: dc * 1024 + tcl * 128 + 128],
                                ident_f[:, :]),
                            wait=wl, inc=("pe" if dc == 7 else None))

            def sc_fn(E, dst, psum, hh):
                return E.tensor_copy(dst, psum[:, :])
            ps_free_t = emit_epilogue_chunk(k, b, tc, [ps[4], ps[5]], sc_fn,
                                            dr["r_in"].ap(), dr["r_out"].ap(),
                                            t_tr, radd_free)
        k.flush()
    k.op("sync", lambda E: E.nop(),
         wait=[("st0", k.tok("st0")), ("st1", k.tok("st1"))])
    k.flush()


def emit_attn1(nc, k, dr, ps, pt, pw):
    a = Arena(nc)
    ident_b = a.alloc("C_ident", [128, 128], BF16, align=32)
    cc1_b = a.alloc("C_cc1", [128, T], BF16, align=32)
    ss1_b = a.alloc("C_ss1", [128, T], BF16, align=32)
    perm_b = a.alloc("C_perm", [128, 128], BF16, align=32)
    tril_b = a.alloc("C_tril", [128, 128], BF16, align=32)
    v_tm = a.alloc("C_v", [128, NTC * 1040], BF16, align=32)
    cq = a.alloc("C_cq", [128, 4 * T], BF16, align=32)
    ckv = a.alloc("C_ckv", [128, 4 * T], BF16, align=32)
    qcp = [a.alloc(f"C_qcp{i}", [128, T], BF16, align=32) for i in range(2)]
    kcp = [a.alloc(f"C_kcp{i}", [128, T], BF16, align=32) for i in range(2)]
    cc1c_b = a.alloc("C_cc1c", [128, T], BF16, align=32)
    ss1c_b = a.alloc("C_ss1c", [128, T], BF16, align=32)
    permc_b = a.alloc("C_permc", [128, 128], BF16, align=32)
    wuq_sb = a.alloc("C_wuq", [128, 4 * 1024], BF16, align=32)
    wuk_sb = a.alloc("C_wuk", [128, 4 * 1024], BF16, align=32)
    wqr_sb = a.alloc("C_wqr", [128, 4 * 1024], BF16, align=32)
    kr_pad = a.alloc("C_kr", [128, T], BF16, align=32)
    attnout = a.alloc("C_ao", [128, NTC * 1024], BF16, align=32)
    ov = a.off
    y2_fm = a.at("C_y2fm", [128, 8 * T], BF16, ov)
    ebuf = [a.at(f"C_e{i}", [128, 16 * 512], BF16, ov + i * 16 * 512 * 2) for i in range(2)]
    ao_fm = a.at("C_aofm", [128, 8 * T], BF16, ov)
    a.off = ov + 8 * T * 2
    wbuf = a.at("C_wbuf", [128, 8 * 512], BF16, a.offs["C_kcp0"])
    wo_sb = a.at("C_wo", [128, 8 * 1024], BF16, a.offs["C_qcp0"])
    scr = a.alloc("C_scr", [128, 1024], BF16, align=32)
    rcp = a.alloc("C_rcp", [128, 2], F32, align=32)
    emask = [a.alloc(f"C_em{i}", [128, 128], BF16, align=32) for i in range(2)]
    b = LNBufs(a, "C")


    for nm, dst in [("ident", ident_b), ("cc1", cc1_b), ("ss1", ss1_b),
                    ("perm1m", perm_b), ("tril", tril_b), ("cc1c", cc1c_b),
                    ("ss1c", ss1c_b), ("perm1c", permc_b)]:
        k.dma("gpsimd", dst[:, :], dr[nm].ap(), sem="scst")
    t_cst = k.tok("scst")
    t_ones = k.op("vector", lambda E: E.memset(
        v_tm[:, :].rearrange("q (h c) -> q h c", c=65)[:, :, 64:65], 1.0), inc="dve")
    k.op("vector", lambda E: E.memset(b.eps[:, :], EPS))
    emit_ln_stream(k, b, pt, dr["r_in"].ap(), y2_fm, None, ident_b,
                   first_wait=[("scst", t_cst)])
    k.flush()

    t_wdq = k.dma("gpsimd", wdst(wbuf, 8, 512), wsrc(dr["wdq"]), sem="sg")
    emit_proj(k, wbuf, 8, 4, y2_fm, T, cq, T, ps[4:6], 4, 512,
              extra_wait=[("sg", t_wdq)])
    t_cq = k.tok("dve")
    t_wdkv = k.dma("gpsimd", wdst(wbuf, 8, 512), wsrc(dr["wdkv"]),
                   wait=[("dve", t_cq)], sem="sg")
    emit_proj(k, wbuf, 8, 4, y2_fm, T, ckv, T, ps[4:6], 4, 512,
              extra_wait=[("sg", t_wdkv)])
    t_ckv = k.tok("dve")
    t_wkr = k.dma("gpsimd", wdst(wbuf, 8, 128), wsrc(dr["wkrp"]),
                  wait=[("dve", t_ckv)], sem="sg")
    emit_proj(k, wbuf, 8, 1, y2_fm, T, kr_pad, T, ps[4:6], 4, 512,
              extra_wait=[("sg", t_wkr)])
    t_kr = k.tok("dve")
    t_krr = emit_rope(k, kr_pad, 1, cc1_b, ss1_b, perm_b, ps[4], scr,
                      first_wait=[("dve", t_kr)], ps_free=t_kr)
    t_wuv = k.dma("gpsimd", wdst(wbuf, 4, 1024), wsrc(dr["wuv"]),
                  wait=[("dve", t_kr)], sem="sg")
    vp_free = [k.tok("dve")] * 2
    for tc in range(NTC):
        for n2 in range(2):
            i = n2
            t_mm = None
            for kc in range(4):
                wl = []
                if kc == 0:
                    wl = [("dve", vp_free[i])]
                    if tc == 0 and n2 == 0:
                        wl += [("sg", t_wuv), ("dve", t_ones), ("dve", t_ckv)]
                t_mm = k.op("tensor",
                            lambda E, i=i, kc=kc, tc=tc, n2=n2: E.matmul(
                                ps[i][:, :], ckv[:, kc * T + tc * 128: kc * T + tc * 128 + 128],
                                wbuf[:, kc * 1024 + n2 * 512: kc * 1024 + n2 * 512 + 512],
                                start=(kc == 0), stop=(kc == 3)),
                            wait=wl, inc=("pe" if kc == 3 else None))

            def evac_v(E, tc=tc, n2=n2, i=i):
                lo = tc * 1040 + n2 * 8 * 65
                dst = v_tm[:, lo:lo + 8 * 65].rearrange("q (h c) -> q h c", c=65)[:, :, 0:64]
                return E.tensor_copy(dst, ps[i][:, :].rearrange("q (a c) -> q a c", a=8))
            vp_free[i] = k.op("vector", evac_v, wait=[("pe", t_mm)], inc="dve")
    k.dma("gpsimd", wdst(wuq_sb, 4, 1024), wsrc(dr["wuq"]), sem="sgr")
    k.dma("gpsimd", wdst(wuk_sb, 4, 1024), wsrc(dr["wuk"]), sem="sgr")
    t_wres = k.dma("gpsimd", wdst(wqr_sb, 4, 1024), wsrc(dr["wqrp"]), sem="sgr")
    k.flush()

    SCALE1 = 1.0 / float(np.sqrt(96.0))
    t_krf = 0
    for hb in range(2):
        k.op("vector", lambda E, hb=hb: E.memset(qcp[hb][96:128, :], 0.0))
        k.op("vector", lambda E, hb=hb: E.memset(kcp[hb][96:128, :], 0.0))
        t_krf = k.dma("sync", kcp[hb][64:96, :], kr_pad[0:32, :],
                      wait=[("dve", t_krr)], sem="krf")
    t_z = k.op("vector", lambda E: E.engine_nop(), inc="dve")
    head_done = [k.tok("pe")] * 2
    proj_done = [k.tok("dve")] * 2
    qk_free = [k.tok("act")] * 2
    e_wfree = [k.tok("pe")] * 2
    pv_free = [k.tok("dve")] * 2
    em_free = [k.tok("pe")] * 2
    ps45_free = k.tok("dve")
    pvw = [pt[0][:, :].bitcast(F32), pt[1][:, :].bitcast(F32)]

    def head_proj(h):
        nonlocal ps45_free
        hb = h % 2
        toks = [ps45_free, ps45_free]
        for tt in range(4):
            i = tt % 2
            for kc in range(4):
                wl = []
                if kc == 0:
                    wl = [("dve", toks[i]), ("pe", head_done[hb])]
                    if h == 0 and tt == 0:
                        wl.append(("sgr", t_wres))
                k.op("tensor",
                     lambda E, i=i, kc=kc, tt=tt, h=h: E.matmul(
                         pvw[i][0:64, :],
                         wuq_sb[:, kc * 1024 + h * 64: kc * 1024 + h * 64 + 64],
                         cq[:, kc * T + tt * 512: kc * T + tt * 512 + 512],
                         start=(kc == 0), stop=(kc == 3)),
                     wait=wl)
            for kc in range(4):
                qrc = kc * 1024 + (h // 2) * 128 + (h % 2) * 64
                t_mm = k.op("tensor",
                            lambda E, i=i, kc=kc, tt=tt, qrc=qrc: E.matmul(
                                pvw[i][64:96, :],
                                wqr_sb[:, qrc: qrc + 32],
                                cq[:, kc * T + tt * 512: kc * T + tt * 512 + 512],
                                start=(kc == 0), stop=(kc == 3)),
                            inc=("pe" if kc == 3 else None))
            toks[i] = k.op("vector",
                           lambda E, tt=tt, hb=hb, i=i: E.tensor_copy(
                               qcp[hb][0:96, tt * 512: tt * 512 + 512], pvw[i][0:96, :]),
                           wait=[("pe", t_mm)], inc="dve")
        for tt in range(4):
            i = tt % 2
            t_mm = None
            for kc in range(4):
                wl = [("dve", toks[i])] if kc == 0 else []
                t_mm = k.op("tensor",
                            lambda E, i=i, kc=kc, tt=tt, h=h: E.matmul(
                                pvw[i][0:64, :],
                                wuk_sb[:, kc * 1024 + h * 64: kc * 1024 + h * 64 + 64],
                                ckv[:, kc * T + tt * 512: kc * T + tt * 512 + 512],
                                start=(kc == 0), stop=(kc == 3)),
                            wait=wl, inc=("pe" if kc == 3 else None))
            toks[i] = k.op("vector",
                           lambda E, tt=tt, hb=hb, i=i: E.tensor_copy(
                               kcp[hb][0:64, tt * 512: tt * 512 + 512], pvw[i][0:64, :]),
                           wait=[("pe", t_mm)], inc="dve")
        t = emit_rope(k, qcp[hb], 1, cc1c_b, ss1c_b, permc_b, pvw[0], scr,
                      first_wait=[("dve", max(toks)), ("dve", t_z)], ps_free=max(toks))
        ps45_free = t
        return t

    exp_tok_c = {}

    def emit_qk_c(h, j):
        hb = h % 2
        par = (4 * h + j) % 2
        nkt = min(4 * j + 4, 16)
        for kt0 in range(0, nkt, 2):
            pb2 = (kt0 // 2) % 2
            t_qk = None
            for kt in (kt0, kt0 + 1):
                wl1 = []
                if kt == kt0:
                    wl1 = [("act", qk_free[pb2])]
                    if j == 0 and kt0 == 0:
                        wl1 += [("dve", proj_done[hb]), ("krf", t_krf)]
                t_qk = k.op("tensor",
                            lambda E, pb2=pb2, kt=kt, kt0=kt0, j=j, hb=hb: E.matmul(
                                pw[pb2][:, (kt - kt0) * 512:(kt - kt0) * 512 + 512],
                                kcp[hb][:, kt * 128: kt * 128 + 128],
                                qcp[hb][:, j * 512: j * 512 + 512],
                                start=True, stop=True),
                            wait=wl1, inc="pe")
            qk_free[pb2] = k.op("scalar",
                                lambda E, pb2=pb2, kt0=kt0, par=par: E.activation(
                                    ebuf[par][:, kt0 * 512: kt0 * 512 + 1024],
                                    pw[pb2][:, :], AF.Exp, scale=SCALE1),
                                wait=[("pe", t_qk),
                                      ("pe", e_wfree[par] if kt0 == 0 else 0)],
                                inc="act")
        exp_tok_c[(h, j)] = qk_free[((nkt - 2) // 2) % 2]
        return k.tok("pe")

    def emit_pv_c(h, j):
        par = (4 * h + j) % 2
        t_exp = exp_tok_c[(h, j)]
        last = None
        for qc in range(4):
            qcg = 4 * j + qc
            nk2 = qcg + 1
            pvp = qc % 2
            emp = qc % 2
            t_em = k.op("vector",
                        lambda E, par=par, qcg=qcg, qc=qc, emp=emp: E.tensor_tensor(
                            emask[emp][:, :],
                            ebuf[par][:, qcg * 512 + qc * 128: qcg * 512 + qc * 128 + 128],
                            tril_b[:, :], ALU.mult),
                        wait=[("act", t_exp), ("pe", em_free[emp])],
                        inc="dve")
            t_pv = None
            for kt in range(nk2):
                lhs = (emask[emp][:, :] if kt == qcg else
                       ebuf[par][:, kt * 512 + qc * 128: kt * 512 + qc * 128 + 128])
                wl = []
                if kt == 0:
                    wl = [("act", t_exp), ("dve", pv_free[pvp]), ("dve", t_em)]
                t_pv = k.op("tensor",
                            lambda E, pvp=pvp, kt=kt, h=h, lhs=lhs, nk2=nk2: E.matmul(
                                ps[4 + pvp][:, 0:65], lhs,
                                v_tm[:, kt * 1040 + 65 * h: kt * 1040 + 65 * h + 65],
                                start=(kt == 0), stop=(kt == nk2 - 1)),
                            wait=wl, inc=("pe" if kt == nk2 - 1 else None))
            em_free[emp] = t_pv
            last = t_pv
            if qc == 3:
                e_wfree[par] = t_pv
            t_rc = k.op("vector", lambda E, pvp=pvp: E.reciprocal(
                rcp[:, pvp:pvp + 1], ps[4 + pvp][:, 64:65]),
                wait=[("pe", t_pv)], inc="dve")
            pv_free[pvp] = k.op("vector",
                                lambda E, pvp=pvp, qcg=qcg, h=h: E.tensor_scalar_mul(
                                    attnout[:, qcg * 1024 + 64 * h: qcg * 1024 + 64 * h + 64],
                                    ps[4 + pvp][:, 0:64], rcp[:, pvp:pvp + 1]),
                                wait=[("dve", t_rc)], inc="dve")
        return last

    proj_done[0] = head_proj(0)
    pending = None
    for h in range(NHEAD):
        if h + 1 < NHEAD:
            proj_done[(h + 1) % 2] = head_proj(h + 1)
        for j in range(4):
            t_lastqk = emit_qk_c(h, j)
            if pending is not None:
                emit_pv_c(*pending)
            pending = (h, j)
            if j == 3:
                head_done[h % 2] = t_lastqk
    emit_pv_c(*pending)
    k.flush()

    t_wo = k.dma("gpsimd", wdst(wo_sb, 8, 1024), wsrc(dr["wot"]), sem="sg")
    tr_free = [k.tok("dve")] * 2
    for tc in range(NTC):
        p = tc % 2
        t_tr = None
        for dc in range(NDC):
            t_tr = k.op("tensor",
                        lambda E, p=p, dc=dc, tc=tc: E.transpose(
                            pt[p][:, dc * 128:(dc + 1) * 128],
                            attnout[:, tc * 1024 + dc * 128: tc * 1024 + dc * 128 + 128],
                            ident_b[:, :]),
                        wait=([("dve", tr_free[p])] if dc == 0 else []),
                        inc=("pe" if dc == NDC - 1 else None))

        def evac_t(E, p=p, tc=tc):
            dst = ao_fm[:, :].rearrange("q (a t) -> q a t", a=8)[:, :, tc * 128:tc * 128 + 128]
            return E.tensor_copy(dst, pt[p][:, :].rearrange("q (a t) -> q a t", a=8))
        tr_free[p] = k.op("vector", evac_t, wait=[("pe", t_tr)], inc="dve")
    t_trd = k.tok("dve")
    radd_free = [0, 0]
    wo_ps_free = [k.tok("dve")] * 2
    for tc in range(NTC):
        t_mm = None
        for kc in range(NDC):
            for n2 in range(2):
                wl = []
                if kc == 0:
                    wl = [("dve", wo_ps_free[n2])]
                    if tc == 0:
                        wl += [("sg", t_wo), ("dve", t_trd)]
                t_mm = k.op("tensor",
                            lambda E, kc=kc, tc=tc, n2=n2: E.matmul(
                                ps[n2][:, :],
                                ao_fm[:, kc * T + tc * 128: kc * T + tc * 128 + 128],
                                wo_sb[:, kc * 1024 + n2 * 512: kc * 1024 + n2 * 512 + 512],
                                start=(kc == 0), stop=(kc == NDC - 1)),
                            wait=wl, inc=("pe" if kc == NDC - 1 and n2 == 1 else None))

        def sc_fn(E, dst, psum, half):
            return E.tensor_copy(dst, psum[:, :])
        t_ev = emit_epilogue_chunk(k, b, tc, [ps[0], ps[1]], sc_fn,
                                   dr["r_in"].ap(), dr["r_out"].ap(), t_mm, radd_free)
        wo_ps_free[0] = t_ev
        wo_ps_free[1] = t_ev
    k.op("sync", lambda E: E.nop(),
         wait=[("st0", k.tok("st0")), ("st1", k.tok("st1"))])
    k.flush()


def emit_ffn1(nc, k, dr, ps, pt):
    a = Arena(nc)
    ident_b = a.alloc("D_ident", [128, 128], BF16)
    sel4_b = a.alloc("D_sel4", [32, 128], BF16)
    ones_b = a.alloc("D_ones", [1, 128], BF16)
    phib_b = a.alloc("D_phib", [1, 2048], BF16)
    ebfm = a.alloc("D_ebfm", [128, 64], F32)
    b = LNBufs(a, "D")
    sp = a.alloc("D_sp", [128, 16 * 64], F32)
    sn = a.alloc("D_sn", [128, 16 * 32], F32)
    snb = a.alloc("D_snb", [128, 512], BF16)
    rsp = a.alloc("D_rsp", [128, 1024], BF16)
    rsp_f = a.alloc("D_rspf", [128, 1024], F32)
    rsnt = a.alloc("D_rsnt", [32, T], BF16)
    rsnt_f = a.alloc("D_rsntf", [32, T], F32)
    rep = a.alloc("D_rep", [128, T], BF16)
    dstage = [a.alloc(f"D_dst{i}", [128, 128], BF16) for i in range(2)]
    ect = a.alloc("D_ect", [128, 16 * T], BF16)
    y3_fm = a.alloc("D_y3f", [128, 8 * T], BF16)
    y3_tm = a.alloc("D_y3t", [128, 16 * 1024], BF16)
    phi_sb = a.alloc("D_phi", [128, 8 * 2048], BF16)
    esl = a.at("D_esl", [128, 8 * 1024], BF16, a.offs["D_y3f"])
    xs2 = a.at("D_xs2", [128, 2 * 1024], F32, a.offs["D_y3f"] + 16384)
    xtanh = a.at("D_xtanh", [128, 2 * 1024], BF16, a.offs["D_y3f"] + 16384 + 8192)
    x_fm = a.at("D_xfm", [128, 8 * 256], BF16, a.offs["D_y3f"] + 16384 + 8192 + 4096)
    ytt_fm = a.at("D_ytt", [128, 8 * 256], BF16, a.offs["D_phi"])
    ytst = a.at("D_ytst", [128, 1024], BF16, a.offs["D_phi"] + 4096)
    yt_np = a.at("D_ytnp", [128, 16 * 1024], BF16, a.offs["D_y3t"])
    ct = ect


    xp_d = nc.dram_tensor("D_xp", [2048, 1024], F32)
    xs_d = nc.dram_tensor("D_xs", [256, 1024], F32)
    ytp_d = nc.dram_tensor("D_ytp", [256, 1024], BF16)
    ytf_d = nc.dram_tensor("D_ytf", [2048, 1024], BF16)

    for nm, dst in [("ident", ident_b), ("sel4", sel4_b), ("ones1", ones_b),
                    ("phib", phib_b), ("ebfm", ebfm)]:
        k.dma("gpsimd", dst[:, :], dr[nm].ap(), sem="scst")
    t_cst = k.tok("scst")
    k.op("vector", lambda E: E.memset(b.eps[:, :], EPS))
    emit_ln_stream(k, b, pt, dr["r_in"].ap(), y3_fm, y3_tm, ident_b,
                   first_wait=[("scst", t_cst)])
    k.flush()

    t_phi = k.dma("gpsimd", wdst(phi_sb, 8, 2048), wsrc(dr["phit"]), sem="sg")
    ps_free = [k.tok("act")] * 4
    sum_free = k.tok("dve")
    for tc in range(NTC):
        t_e = [None] * 4
        for n in range(4):
            wl = [("act", ps_free[n])]
            if tc == 0 and n == 0:
                wl.append(("sg", t_phi))
            k.op("tensor",
                 lambda E, n=n: E.matmul(
                     ps[n][:, :], ones_b[0:1, :], phib_b[0:1, n * 512:(n + 1) * 512],
                     start=True, stop=False),
                 wait=wl)
            t_mm = None
            for kc in range(NDC):
                t_mm = k.op("tensor",
                            lambda E, n=n, kc=kc, tc=tc: E.matmul(
                                ps[n][:, :],
                                y3_fm[:, kc * T + tc * 128: kc * T + tc * 128 + 128],
                                phi_sb[:, kc * 2048 + n * 512: kc * 2048 + n * 512 + 512],
                                start=False, stop=(kc == NDC - 1)),
                            inc=("pe" if kc == NDC - 1 else None))
            t_e[n] = k.op("scalar",
                          lambda E, n=n, tc=tc: E.activation(
                              ect[:, tc * 2048 + n * 512: tc * 2048 + n * 512 + 512],
                              ps[n][:, :], AF.Exp),
                          wait=[("pe", t_mm)], inc="act")
            ps_free[n] = t_e[n]
        ev = ect[:, tc * 2048:(tc + 1) * 2048]
        k.op("vector",
             lambda E, ev=ev, tc=tc: E.reduce_sum(
                 sp[:, tc * 64:(tc + 1) * 64],
                 ev.rearrange("q (n p2) -> q n p2", p2=32), axis=AX.X),
             wait=[("act", t_e[3]), ("dve", sum_free)])
        sum_free = k.op("vector",
                        lambda E, ev=ev, tc=tc: E.reduce_sum(
                            sn[:, tc * 32:(tc + 1) * 32],
                            ev.rearrange("q (n p2) -> q p2 n", p2=32), axis=AX.X),
                        inc="dve")
    k.op("vector", lambda E: E.reciprocal(rsp_f[:, :], sp[:, :]), drain=True)
    t_rsp = k.op("vector", lambda E: E.tensor_copy(rsp[:, :], rsp_f[:, :]),
                 inc="dve", drain=True)
    t_snb = k.op("vector", lambda E: E.tensor_copy(snb[:, :], sn[:, :]),
                 inc="dve", drain=True)
    t_tr = None
    for tc in range(NTC):
        t_tr = k.op("tensor",
                    lambda E, tc=tc: E.transpose(
                        pt[tc // 8][0:32, (tc % 8) * 128:(tc % 8) * 128 + 128],
                        snb[:, tc * 32:(tc + 1) * 32], ident_b[:, :]),
                    wait=([("dve", t_snb)] if tc == 0 else []),
                    inc=("pe" if tc == NTC - 1 else None))
    k.op("vector", lambda E: E.tensor_copy(rsnt[0:32, 0:1024], pt[0][0:32, :]),
         wait=[("pe", t_tr)], drain=True)
    k.op("vector", lambda E: E.tensor_copy(rsnt[0:32, 1024:2048], pt[1][0:32, :]),
         drain=True)
    k.op("vector", lambda E: E.tensor_copy(rsnt_f[0:32, :], rsnt[0:32, :]), drain=True)
    k.op("vector", lambda E: E.reciprocal(rsnt_f[0:32, :], rsnt_f[0:32, :]), drain=True)
    t_rsnt = k.op("vector", lambda E: E.tensor_copy(rsnt[0:32, :], rsnt_f[0:32, :]),
                  inc="dve", drain=True)
    t_rep = t_rsnt
    for n in range(4):
        t_sel = k.op("tensor",
                     lambda E, n=n: E.matmul(ps[4][:, :], sel4_b[0:32, :],
                                             rsnt[0:32, n * 512:(n + 1) * 512],
                                             start=True, stop=True),
                     wait=[("dve", t_rep)], inc="pe")
        t_rep = k.op("vector",
                     lambda E, n=n: E.tensor_copy(rep[:, n * 512:(n + 1) * 512],
                                                  ps[4][:, :]),
                     wait=[("pe", t_sel)], inc="dve", drain=True)
    k.flush()

    xstore_free = [0, 0]
    dpar_free = [k.tok("pe")] * 2
    for npc in range(NTC):
        pxp = npc % 2
        pa, pb2 = 2 + 2 * pxp, 3 + 2 * pxp
        t_mm = None
        for tc in range(NTC):
            dpar = tc % 2
            t_dst = k.op("vector",
                         lambda E, tc=tc, npc=npc, dpar=dpar: E.tensor_tensor(
                             dstage[dpar][:, :].rearrange("q (n p2) -> q n p2", p2=32),
                             ect[:, tc * 2048 + npc * 128: tc * 2048 + npc * 128 + 128]
                             .rearrange("q (n p2) -> q n p2", p2=32),
                             rsp[:, tc * 64 + npc * 4: tc * 64 + npc * 4 + 4][:, :, None]
                             .broadcast_to([128, 4, 32]),
                             ALU.mult),
                         wait=[("pe", dpar_free[dpar])], inc="dve")
            for n2 in range(2):
                wl = [("dve", t_dst)]
                if tc == 0 and n2 == 0:
                    wl.append((f"xp{pxp}", xstore_free[pxp]))
                t_mm = k.op("tensor",
                            lambda E, tc=tc, n2=n2, pp=(pa if n2 == 0 else pb2), dpar=dpar:
                            E.matmul(
                                ps[pp][:, :], dstage[dpar][:, :],
                                y3_tm[:, tc * 1024 + n2 * 512: tc * 1024 + n2 * 512 + 512],
                                start=(tc == 0), stop=(tc == NTC - 1)),
                            wait=wl, inc=("pe" if n2 == 1 else None))
            dpar_free[dpar] = t_mm
        k.op("vector", lambda E, pa=pa, pxp=pxp: E.tensor_copy(
            b.rb[pxp][:, 0:512], ps[pa][:, :]),
            wait=[("pe", t_mm)], drain=True)
        t_x = k.op("vector", lambda E, pb2=pb2, pxp=pxp: E.tensor_copy(
            b.rb[pxp][:, 512:1024], ps[pb2][:, :]),
            inc="dve", drain=True)
        xstore_free[pxp] = k.dma("sync", xp_d.ap()[npc * 128:(npc + 1) * 128, :],
                                 b.rb[pxp][:, :], wait=[("dve", t_x)], sem=f"xp{pxp}")
    k.op("gpsimd", lambda E: E.collective_compute(
        "ReduceScatter", ALU.add, replica_groups=[list(range(NCORES))],
        ins=[xp_d.ap().opt()], outs=[xs_d.ap().opt()]),
        wait=[("xp0", k.tok("xp0")), ("xp1", k.tok("xp1"))], inc="cc")
    ctp_free = [k.tok("dve")] * 2
    pairs = [(i, j) for i in range(NTC) for j in range(i, NTC)]
    for idx, (i, j) in enumerate(pairs):
        pq = idx % 2
        t1 = k.op("tensor",
                  lambda E, i=i, j=j, pq=pq: E.transpose(
                      pt[pq][:, 0:128],
                      ect[:, i * 2048 + j * 128: i * 2048 + j * 128 + 128],
                      ident_b[:, :]),
                  wait=[("dve", ctp_free[pq])],
                  inc=("pe" if i == j else None))
        if i != j:
            t1 = k.op("tensor",
                      lambda E, i=i, j=j, pq=pq: E.transpose(
                          pt[pq][:, 128:256],
                          ect[:, j * 2048 + i * 128: j * 2048 + i * 128 + 128],
                          ident_b[:, :]),
                      inc="pe")
        tlast = k.op("vector",
                     lambda E, i=i, j=j, pq=pq: E.tensor_tensor(
                         ect[:, j * 2048 + i * 128: j * 2048 + i * 128 + 128],
                         pt[pq][:, 0:128], rep[:, i * 128: i * 128 + 128], ALU.mult),
                     wait=[("pe", t1)], inc=("dve" if i == j else None))
        if i != j:
            tlast = k.op("vector",
                         lambda E, i=i, j=j, pq=pq: E.tensor_tensor(
                             ect[:, i * 2048 + j * 128: i * 2048 + j * 128 + 128],
                             pt[pq][:, 128:256], rep[:, j * 128: j * 128 + 128], ALU.mult),
                         inc="dve")
        ctp_free[pq] = tlast
    t_ct = k.tok("dve")
    k.flush()

    t_xs = k.dma("sync", xs2[:, :].rearrange("q (c d) -> q c d", c=2),
                 xs_d.ap().rearrange("(c q) d -> q c d", q=128),
                 wait=[("cc", 1)], sem="xs")
    k.op("vector", lambda E: E.tensor_scalar(xs2[:, :], xs2[:, :], -33000.0, 65000.0,
                                             ALU.max, ALU.min),
         wait=[(f"xs", t_xs)])
    t_xfm = k.tok("dve")
    for c2 in range(2):
        sl = xs2[:, c2 * 1024:(c2 + 1) * 1024]
        k.op("vector", lambda E, sl=sl: E.bn_stats(b.st[:, 0:6], sl[:, 0:512]),
             drain=True)
        k.op("vector", lambda E, sl=sl: E.bn_stats(b.st[:, 6:12], sl[:, 512:1024]))
        t_st = k.op("vector", lambda E: E.bn_aggr(b.mv[:, 0:2], b.st[:, :]),
                    inc="dve", drain=True)
        k.op("scalar", lambda E: E.activation(b.rstd[:, 0:1], b.mv[:, 1:2], AF.Ln,
                                              bias=b.eps[:, 0:1]),
             wait=[("dve", t_st)])
        t_rs = k.op("scalar", lambda E: E.activation(b.rstd[:, 1:2], b.rstd[:, 0:1],
                                                     AF.Exp, scale=-0.5),
                    inc="act", drain=True)
        t_ap2 = k.op("vector", lambda E, sl=sl: E.tensor_scalar(
            sl, sl, b.mv[:, 0:1], b.rstd[:, 1:2], ALU.subtract, ALU.mult),
            wait=[("act", t_rs)], inc="dve", drain=True)
        t_th = k.op("scalar",
                    lambda E, c2=c2, sl=sl: E.activation(
                        xtanh[:, c2 * 1024:(c2 + 1) * 1024], sl, AF.Tanh),
                    wait=[("dve", t_ap2)], inc="act", drain=True)
        t_trx = None
        for dc in range(NDC):
            t_trx = k.op("tensor",
                         lambda E, c2=c2, dc=dc: E.transpose(
                             pt[c2][:, dc * 128:(dc + 1) * 128],
                             xtanh[:, c2 * 1024 + dc * 128: c2 * 1024 + dc * 128 + 128],
                             ident_b[:, :]),
                         wait=([("act", t_th), ("dve", t_ct)] if dc == 0 else []),
                         inc=("pe" if dc == NDC - 1 else None))

        def evac_x(E, c2=c2):
            dst = x_fm[:, :].rearrange("q (a t) -> q a t", a=8)[:, :, c2 * 128:c2 * 128 + 128]
            return E.tensor_copy(dst, pt[c2][:, :].rearrange("q (a t) -> q a t", a=8))
        t_xfm = k.op("vector", evac_x, wait=[("pe", t_trx)], inc="dve")
    es_free = k.tok("pe")
    ytt_done = None
    for e in range(8):
        t_es = k.dma("gpsimd", wdst(esl, 8, 1024),
                     dr["ew"].ap()[e * 1024:(e + 1) * 1024, :]
                     .rearrange("(c p) m -> p c m", p=128),
                     wait=[("pe", es_free)], sem="es")
        t_pe_last = None
        evtok = [k.tok("dve")] * 2
        for hc in range(8):
            pp = hc % 2
            t_mm = None
            for kc in range(NDC):
                wl = []
                if kc == 0:
                    wl = [("dve", evtok[pp]), ("es", t_es)]
                    if e == 0 and hc == 0:
                        wl.append(("dve", t_xfm))
                t_mm = k.op("tensor",
                            lambda E, e=e, hc=hc, kc=kc, pp=pp: E.matmul(
                                ps[pp][:, 0:32],
                                esl[:, kc * 1024 + hc * 128: kc * 1024 + hc * 128 + 128],
                                x_fm[:, kc * 256 + 32 * e: kc * 256 + 32 * e + 32],
                                start=(kc == 0), stop=(kc == NDC - 1)),
                            wait=wl, inc=("pe" if kc == NDC - 1 else None))
            evtok[pp] = k.op("vector",
                             lambda E, e=e, hc=hc, pp=pp: E.tensor_scalar(
                                 ytt_fm[:, hc * 256 + e * 32: hc * 256 + e * 32 + 32],
                                 ps[pp][:, 0:32], ebfm[:, e * 8 + hc:e * 8 + hc + 1],
                                 None, ALU.add),
                             wait=[("pe", t_mm)], inc="dve")
            t_pe_last = t_mm
        es_free = t_pe_last
    t_ytt = k.tok("dve")
    for c2 in range(2):
        t_try = None
        for hc in range(8):
            t_try = k.op("tensor",
                         lambda E, c2=c2, hc=hc: E.transpose(
                             pt[c2][:, hc * 128:(hc + 1) * 128],
                             ytt_fm[:, hc * 256 + c2 * 128: hc * 256 + c2 * 128 + 128],
                             ident_b[:, :]),
                         wait=([("dve", t_ytt)] if hc == 0 else []),
                         inc=("pe" if hc == NDC - 1 else None))
        t_ys = k.op("vector",
                    lambda E, c2=c2: E.tensor_copy(ytst[:, :], pt[c2][:, :]),
                    wait=[("pe", t_try)], inc="dve")
        k.dma("sync", ytp_d.ap()[c2 * 128:(c2 + 1) * 128, :], ytst[:, :],
              wait=[("dve", t_ys)], sem="yt")
        t_ytt = k.op("vector", lambda E, c2=c2: E.engine_nop(),
                     wait=[("yt", k.tok("yt"))], inc="dve")
    t_yt = k.tok("yt")
    k.op("gpsimd", lambda E: E.collective_compute(
        "AllGather", ALU.bypass, replica_groups=[list(range(NCORES))],
        ins=[ytp_d.ap().opt()], outs=[ytf_d.ap().opt()]),
        wait=[("yt", t_yt)], inc="cc")
    t_ytl = k.dma("sync", yt_np[:, :].rearrange("q (c d) -> q c d", c=16),
                  ytf_d.ap().rearrange("(c q) d -> q c d", q=128),
                  wait=[("cc", 2)], sem="ytl")
    k.flush()

    radd_free = [0, 0]
    y_ps_free = [k.tok("dve")] * 2
    for tc in range(NTC):
        t_mm = None
        for npc in range(NTC):
            for n2 in range(2):
                wl = []
                if npc == 0:
                    wl = [("dve", y_ps_free[n2])]
                    if tc == 0:
                        wl += [("ytl", t_ytl), ("dve", t_ct)]
                t_mm = k.op("tensor",
                            lambda E, npc=npc, tc=tc, n2=n2: E.matmul(
                                ps[n2][:, :],
                                ct[:, npc * 2048 + tc * 128: npc * 2048 + tc * 128 + 128],
                                yt_np[:, npc * 1024 + n2 * 512: npc * 1024 + n2 * 512 + 512],
                                start=(npc == 0), stop=(npc == NTC - 1)),
                            wait=wl, inc=("pe" if npc == NTC - 1 and n2 == 1 else None))

        def sc_fn(E, dst, psum, half):
            return E.tensor_copy(dst, psum[:, :])
        t_ev = emit_epilogue_chunk(k, b, tc, [ps[0], ps[1]], sc_fn,
                                   dr["r_in"].ap(), dr["r_out"].ap(), t_mm, radd_free)
        y_ps_free[0] = t_ev
        y_ps_free[1] = t_ev
    k.op("sync", lambda E: E.nop(),
         wait=[("st0", k.tok("st0")), ("st1", k.tok("st1"))])
    k.flush()



def _mk_psums(nc):
    pw = [nc.alloc_psum_tensor(f"pw{i}", [128, 1024], F32) for i in range(3)]
    ptw = nc.alloc_psum_tensor("ptw", [128, 2048], BF16)
    ps = []
    for i in range(3):
        ps.append(pw[i][:, 0:512])
        ps.append(pw[i][:, 512:1024])
    pt = [ptw[:, 0:1024], ptw[:, 1024:2048]]
    return ps, pt, pw

A_WNAMES = ["wq", "wdn", "wku", "wvu", "wo", "ident", "cc0", "ss0", "perm0m"]
B_WNAMES = ["w1t", "w2t", "h1t", "h2t", "w3t", "s1", "s2", "hs1", "hs2", "s3fm", "ident"]


def _phase_a_nc(common):
    nc = bass.Bass("TRN2", target_bir_lowering=False)
    dr = {}
    dr["x"] = nc.dram_tensor("x", [T, DM], F32, kind="ExternalInput")
    for nm in A_WNAMES:
        dr[nm] = nc.dram_tensor(nm, list(common[nm].shape), F32, kind="ExternalInput")
    dr["r_out"] = nc.dram_tensor("r_out", [T, DM], F32, kind="ExternalOutput")
    k = K(nc)
    ps, pt, pw = _mk_psums(nc)
    emit_attn0(nc, k, dr, ps, pt, pw)
    return nc


def _phase_b_nc(common):
    nc = bass.Bass("TRN2", target_bir_lowering=False)
    dr = {}
    dr["r_in"] = nc.dram_tensor("r_in", [T, DM], F32, kind="ExternalInput")
    for nm in B_WNAMES:
        dr[nm] = nc.dram_tensor(nm, list(common[nm].shape), F32, kind="ExternalInput")
    dr["r_out"] = nc.dram_tensor("r_out", [T, DM], F32, kind="ExternalOutput")
    k = K(nc)
    ps, pt, pw = _mk_psums(nc)
    emit_ffn0(nc, k, dr, ps, pt)
    return nc


C_WNAMES = ["wdq", "wuq", "wqrp", "wdkv", "wuk", "wuv", "wkrp", "wot",
            "ident", "cc1", "ss1", "perm1m", "tril", "cc1c", "ss1c", "perm1c"]


def _phase_c_nc(common):
    nc = bass.Bass("TRN2", target_bir_lowering=False)
    dr = {}
    dr["r_in"] = nc.dram_tensor("r_in", [T, DM], F32, kind="ExternalInput")
    for nm in C_WNAMES:
        dr[nm] = nc.dram_tensor(nm, list(common[nm].shape), F32, kind="ExternalInput")
    dr["r_out"] = nc.dram_tensor("r_out", [T, DM], F32, kind="ExternalOutput")
    k = K(nc)
    ps, pt, pw = _mk_psums(nc)
    emit_attn1(nc, k, dr, ps, pt, pw)
    return nc


D_WNAMES = ["phit", "phib", "ident", "sel4", "ones1", "ebfm", "ew"]


def _phase_d_nc(common, shapes_percore):
    nc = bass.Bass("TRN2", target_bir_lowering=False)
    dr = {}
    dr["r_in"] = nc.dram_tensor("r_in", [T, DM], F32, kind="ExternalInput")
    for nm in ["phit", "phib", "ident", "sel4", "ones1"]:
        dr[nm] = nc.dram_tensor(nm, list(common[nm].shape), F32, kind="ExternalInput")
    dr["ebfm"] = nc.dram_tensor("ebfm", [128, 64], F32, kind="ExternalInput")
    dr["ew"] = nc.dram_tensor("ew", [8192, 1024], F32, kind="ExternalInput")
    dr["r_out"] = nc.dram_tensor("r_out", [T, DM], F32, kind="ExternalOutput")
    k = K(nc)
    ps, pt, pw = _mk_psums(nc)
    emit_ffn1(nc, k, dr, ps, pt)
    return nc


N_PHASES = 4


def _fused_nc(common):
    nc = bass.Bass("TRN2", target_bir_lowering=False)
    dr = {}
    dr["x"] = nc.dram_tensor("x", [T, DM], F32, kind="ExternalInput")
    for nm in set(A_WNAMES + B_WNAMES + C_WNAMES + ["phit", "phib", "sel4",
                                                    "ones1", "cc1", "ss1"]):
        dr[nm] = nc.dram_tensor(nm, list(common[nm].shape), F32, kind="ExternalInput")
    dr["ebfm"] = nc.dram_tensor("ebfm", [128, 64], F32, kind="ExternalInput")
    dr["ew"] = nc.dram_tensor("ew", [8192, 1024], F32, kind="ExternalInput")
    r1 = nc.dram_tensor("r1", [T, DM], F32)
    r2 = nc.dram_tensor("r2", [T, DM], F32)
    r3 = nc.dram_tensor("r3", [T, DM], F32)
    out = nc.dram_tensor("out", [T, DM], F32, kind="ExternalOutput")
    k = K(nc)
    ps, pt, pw = _mk_psums(nc)
    da = dict(dr); da["r_out"] = r1
    with nc.named_scope("phA"):
        emit_attn0(nc, k, da, ps, pt, pw)
    db = dict(dr); db["r_in"] = r1; db["r_out"] = r2
    with nc.named_scope("phB"):
        emit_ffn0(nc, k, db, ps, pt)
    dc = dict(dr); dc["r_in"] = r2; dc["r_out"] = r3
    with nc.named_scope("phC"):
        emit_attn1(nc, k, dc, ps, pt, pw)
    dd = dict(dr); dd["r_in"] = r3; dd["r_out"] = out
    with nc.named_scope("phD"):
        emit_ffn1(nc, k, dd, ps, pt)
    return nc


FUSED = True


def kernel(**inputs):
    common, per_core = host_prep(inputs)
    if FUSED:
        nc = _fused_nc(common)
        names = set(A_WNAMES + B_WNAMES + C_WNAMES + ["phit", "phib", "sel4",
                                                      "ones1", "cc1", "ss1"])
        in_maps = []
        for c in range(NCORES):
            m = {"x": per_core[c]["x"], "ebfm": per_core[c]["ebfm"],
                 "ew": per_core[c]["ew"]}
            for nm in names:
                m[nm] = common[nm]
            in_maps.append(m)
        res = run_bass_kernel_spmd(nc, in_maps, core_ids=list(range(NCORES)))
        return np.stack([res.results[c]["out"] for c in range(NCORES)], axis=0)
    return _kernel_multilaunch(common, per_core)


def _kernel_multilaunch(common, per_core):
    nc = _phase_a_nc(common)
    in_maps = []
    for c in range(NCORES):
        m = {"x": per_core[c]["x"]}
        for nm in A_WNAMES:
            m[nm] = common[nm]
        in_maps.append(m)
    res = run_bass_kernel_spmd(nc, in_maps, core_ids=list(range(NCORES)))
    r = [res.results[c]["r_out"] for c in range(NCORES)]
    if N_PHASES == 1:
        return np.stack(r, axis=0)
    nc = _phase_b_nc(common)
    in_maps = []
    for c in range(NCORES):
        m = {"r_in": r[c]}
        for nm in B_WNAMES:
            m[nm] = common[nm]
        in_maps.append(m)
    res = run_bass_kernel_spmd(nc, in_maps, core_ids=list(range(NCORES)))
    r = [res.results[c]["r_out"] for c in range(NCORES)]
    if N_PHASES == 2:
        return np.stack(r, axis=0)
    nc = _phase_c_nc(common)
    in_maps = []
    for c in range(NCORES):
        m = {"r_in": r[c]}
        for nm in C_WNAMES:
            m[nm] = common[nm]
        in_maps.append(m)
    res = run_bass_kernel_spmd(nc, in_maps, core_ids=list(range(NCORES)))
    r = [res.results[c]["r_out"] for c in range(NCORES)]
    if N_PHASES == 3:
        return np.stack(r, axis=0)
    nc = _phase_d_nc(common, None)
    in_maps = []
    for c in range(NCORES):
        m = {"r_in": r[c], "ebfm": per_core[c]["ebfm"], "ew": per_core[c]["ew"]}
        for nm in ["phit", "phib", "ident", "sel4", "ones1"]:
            m[nm] = common[nm]
        in_maps.append(m)
    res = run_bass_kernel_spmd(nc, in_maps, core_ids=list(range(NCORES)))
    r = [res.results[c]["r_out"] for c in range(NCORES)]
    return np.stack(r, axis=0)
